# version 06
# baseline (speedup 1.0000x reference)
import numpy as np
import concourse.bass as bass
import concourse.mybir as mybir
from concourse.bass_utils import run_bass_kernel_spmd
from contextlib import ExitStack

F32 = mybir.dt.float32
BF16 = mybir.dt.bfloat16
I32 = mybir.dt.int32
FP8 = mybir.dt.float8e4
AF = mybir.ActivationFunctionType
ALU = mybir.AluOpType
AX = mybir.AxisListType

NCORES = 8
TOK = 1024
D = 2048
H = 16
HD = 128
CH = 128
NCHUNK = TOK // CH
KC = D // 128
NTILE = TOK // 128
FF = 5632
FF2 = 2 * FF
EPS = 1e-5
INV_D = 1.0 / D

GATE_ELEMS = FF2 * D
DOWN_ELEMS = D * FF
GATE_CHUNKS = GATE_ELEMS // 65536
DOWN_CHUNKS = DOWN_ELEMS // 65536
SH_GCH = GATE_CHUNKS // NCORES
SH_DCH = DOWN_CHUNKS // NCORES
SH_CH = SH_GCH + SH_DCH
GATE_PANELS = FF2 // 512
DOWN_PANELS = D // 512
DOWN_KC = FF // 128
FFN_SLABS = 4
SLAB = TOK // FFN_SLABS
SLAB_TILES = SLAB // 128


def _host_consts(core):
    r = np.exp(-(2.0 ** (-0.5 * (np.arange(H, dtype=np.float64) + 1.0))))
    i = np.arange(CH, dtype=np.float64)
    qdec = r[:, None] ** (i[None, :] + 1.0)
    kdec = r[:, None] ** (CH - 1.0 - i[None, :])
    diff = i[:, None] - i[None, :]
    dmask = np.where(diff[None] >= 0, r[:, None, None] ** np.maximum(diff[None], 0.0), 0.0)
    dmaskT = np.transpose(dmask, (0, 2, 1))
    rC = r ** float(CH)

    dmask4 = np.zeros((4, 128, 512), dtype=np.float32)
    qdec4 = np.zeros((4, 128, 512), dtype=np.float32)
    for g in range(4):
        for hq in range(4):
            h = 4 * g + hq
            dmask4[g, :, hq * 128:(hq + 1) * 128] = dmaskT[h].astype(np.float32)
            qdec4[g, :, hq * 128:(hq + 1) * 128] = np.broadcast_to(
                qdec[h][None, :], (128, 128)).astype(np.float32)
    kdecb = np.zeros((128, D), dtype=np.float32)
    for h in range(H):
        kdecb[:, h * 128:(h + 1) * 128] = kdec[h][:, None].astype(np.float32)

    ex = np.linspace(0.0, 1.0, HD // 4, dtype=np.float32)
    freq = np.power(np.float32(1.0 / 1024.0), ex).astype(np.float32)
    t_abs = (core * TOK + np.arange(TOK)).astype(np.float32)
    theta = (t_abs[:, None] * freq[None, :]).astype(np.float32)
    cosv = np.cos(theta.astype(np.float64)).astype(np.float32)
    sinv = np.sin(theta.astype(np.float64)).astype(np.float32)
    cosrep = np.ascontiguousarray(np.tile(cosv, (1, H)).reshape(NCHUNK, 128, H * 32))
    sinrep = np.ascontiguousarray(np.tile(sinv, (1, H)).reshape(NCHUNK, 128, H * 32))

    zw = np.zeros((128, 128), dtype=np.float32)
    for cp in range(core):
        for h in range(H):
            zw[:, cp * 16 + h] = np.float32(rC[h] ** (8.0 * (core - 1 - cp)))
    rc_pow = np.zeros((H, NCHUNK), dtype=np.float64)
    for h in range(H):
        for l in range(NCHUNK):
            rc_pow[h, l] = rC[h] ** l
    def fl(a):
        a = np.asarray(a)
        return np.where(np.abs(a) < 1.2e-38, 0.0, a)
    return dict(dmask4=fl(dmask4).astype(np.float32), qdec4=fl(qdec4).astype(np.float32),
                kdecb=fl(kdecb).astype(np.float32), cosrep=cosrep, sinrep=sinrep,
                zw=fl(zw).astype(np.float32), rc_pow=fl(rc_pow), rC=fl(rC))


def _prep(inputs):
    x_full = np.ascontiguousarray(np.asarray(inputs["L0_attn_x"], dtype=np.float32)[0])
    Wq = np.asarray(inputs["L0_attn_Wq"], dtype=np.float32)
    Wk = np.asarray(inputs["L0_attn_Wk"], dtype=np.float32)
    Wv = np.asarray(inputs["L0_attn_Wv"], dtype=np.float32)
    Wo = np.asarray(inputs["L0_attn_Wo"], dtype=np.float32)
    rms_w = np.asarray(inputs["L0_attn_rms_w"], dtype=np.float32)
    wgu = np.asarray(inputs["L0_ffn_w_gate_up"], dtype=np.float32)
    wd = np.asarray(inputs["L0_ffn_w_down"], dtype=np.float32)
    wnorm = np.asarray(inputs["L0_ffn_norm_weight"], dtype=np.float32)

    def tile_w(W):
        return np.ascontiguousarray(W.reshape(KC, 128, 4, 512).transpose(2, 0, 1, 3))

    wq_t, wk_t, wv_t, wo_t = tile_w(Wq), tile_w(Wk), tile_w(Wv), tile_w(Wo)
    wguT = np.ascontiguousarray(wgu.T)
    wgu_t = np.ascontiguousarray(wguT.reshape(KC, 128, GATE_PANELS, 512).transpose(2, 0, 1, 3))
    wdT = np.ascontiguousarray(wd.T)
    wd_t = np.ascontiguousarray(wdT.reshape(DOWN_KC, 128, DOWN_PANELS, 512).transpose(2, 0, 1, 3))
    gate_ch = wgu_t.reshape(GATE_CHUNKS, 128, 512)
    down_ch = wd_t.reshape(DOWN_CHUNKS, 128, 512)

    rms_pp = np.ascontiguousarray(rms_w.reshape(KC, 128).T)
    wnorm_rep = np.ascontiguousarray(np.broadcast_to(wnorm[None, :], (128, D)))
    ident = np.eye(128, dtype=np.float32)

    in_maps = []
    for c in range(NCORES):
        cc = _host_consts(c)
        shard = np.ascontiguousarray(np.concatenate([
            gate_ch[c * SH_GCH:(c + 1) * SH_GCH],
            down_ch[c * SH_DCH:(c + 1) * SH_DCH]], axis=0))
        in_maps.append({
            "xin": np.array(x_full[c * TOK:(c + 1) * TOK], copy=True),
            "wq_t": wq_t, "wk_t": wk_t, "wv_t": wv_t, "wo_t": wo_t,
            "wshard": shard, "rms_pp": rms_pp, "wnorm_rep": wnorm_rep, "ident": ident,
            "dmask4": cc["dmask4"], "qdec4": cc["qdec4"], "kdecb": cc["kdecb"],
            "cosrep": cc["cosrep"], "sinrep": cc["sinrep"], "zw": cc["zw"],
        })
    return in_maps


class Trk:
    __slots__ = ("sem", "n")

    def __init__(self, sem):
        self.sem = sem
        self.n = 0

    def inc(self, instr, k=1):
        instr.then_inc(self.sem, k)
        self.n += k
        return self.n


class Alloc:
    def __init__(self, arena, total):
        self.arena, self.total, self.off, self.marks = arena, total, 0, []

    def take(self, n_fp32, dtype=F32):
        n_fp32 = (n_fp32 + 127) & ~127
        assert self.off + n_fp32 <= self.total, f"arena overflow {self.off}+{n_fp32}>{self.total}"
        ap = self.arena[:, self.off:self.off + n_fp32]
        self.off += n_fp32
        return ap if dtype == F32 else ap.bitcast(dtype)

    def push(self):
        self.marks.append(self.off)

    def pop(self):
        self.off = self.marks.pop()


class Builder:
    def __init__(self, upto="F", debug=()):
        self.upto = {"L": 0, "W": 1, "P": 2, "A": 3, "C": 4, "F": 5}[upto]
        self.dbg = set(debug)
        self.DBG = {}
        self.rc = _host_consts(0)

    def declare(self, nc):
        d = lambda *a, **k: nc.dram_tensor(*a, **k).ap()
        self.xin = d("xin", [TOK, D], F32, kind="ExternalInput")
        self.w_t = {
            "q": d("wq_t", [4, KC, 128, 512], F32, kind="ExternalInput"),
            "k": d("wk_t", [4, KC, 128, 512], F32, kind="ExternalInput"),
            "v": d("wv_t", [4, KC, 128, 512], F32, kind="ExternalInput"),
            "o": d("wo_t", [4, KC, 128, 512], F32, kind="ExternalInput"),
        }
        self.wshard = d("wshard", [SH_CH, 128, 512], F32, kind="ExternalInput")
        self.rms_pp_d = d("rms_pp", [128, KC], F32, kind="ExternalInput")
        self.wnorm_d = d("wnorm_rep", [128, D], F32, kind="ExternalInput")
        self.ident_d = d("ident", [128, 128], F32, kind="ExternalInput")
        self.dmask4_d = d("dmask4", [4, 128, 512], F32, kind="ExternalInput")
        self.qdec4_d = d("qdec4", [4, 128, 512], F32, kind="ExternalInput")
        self.kdecb_d = d("kdecb", [128, D], F32, kind="ExternalInput")
        self.cosrep_d = d("cosrep", [NCHUNK, 128, 512], F32, kind="ExternalInput")
        self.sinrep_d = d("sinrep", [NCHUNK, 128, 512], F32, kind="ExternalInput")
        self.zw_d = d("zw", [128, 128], F32, kind="ExternalInput")
        self.yout = d("yout", [TOK, D], F32, kind="ExternalOutput")

        self.xq_sp = nc.dram_tensor("xq_sp", [128, KC * TOK], BF16).ap()
        self.qkv_sp = {nm: nc.dram_tensor(nm + "_sp", [TOK, D], F32).ap() for nm in ("q", "k", "v")}
        self.attn_sp = nc.dram_tensor("attn_sp", [NCHUNK, 128, D], F32).ap()
        self.qd_sp = nc.dram_tensor("qd_sp", [NCHUNK, 128, D], F32).ap()
        self.ar_in = nc.dram_tensor("ar_in", [1, 16], F32).ap()
        self.ar_out = nc.dram_tensor("ar_out", [1, 16], F32).ap()
        self.tern_in = nc.dram_tensor("tern_in", [SH_CH, 128, 512], FP8).ap()
        self.tern_out = nc.dram_tensor("tern_out", [NCORES * SH_CH, 128, 512], FP8).ap()
        self.z_in = nc.dram_tensor("z_in", [128, D], F32).ap()
        self.z_out = nc.dram_tensor("z_out", [NCORES * 128, D], F32).ap()

        for name, shape in [("xh", [128, KC * TOK]), ("xl", [128, KC * TOK]),
                            ("xq", [128, KC * TOK]), ("stats", [128, 96]),
                            ("swv", [1, 16]), ("qkv", [3, TOK, D]),
                            ("attn", [NCHUNK, 128, D]), ("z", [NCORES * 128, D]),
                            ("attnc", [NCHUNK, 128, D]), ("py", [TOK, D]),
                            ("rr", [128, 16]), ("rrow", [1, 2048]),
                            ("tern", [SH_CH, 128, 512]), ("hid", [SLAB, FF])]:
            if name in self.dbg:
                self.DBG[name] = nc.dram_tensor("dbg_" + name, shape, F32, kind="ExternalOutput").ap()

    def build(self):
        nc = bass.Bass("TRN2", target_bir_lowering=False)
        self.nc = nc
        self.declare(nc)
        with ExitStack() as stack:
            E = stack.enter_context
            ARENA_N = 50176
            self.arena = E(nc.sbuf_tensor("arena", [128, ARENA_N], F32))[:]
            self.al = Alloc(self.arena, ARENA_N)
            self.ident = E(nc.sbuf_tensor("identt", [128, 128], F32))[:]
            self.ident_bf = E(nc.sbuf_tensor("identb", [128, 64], F32))[:].bitcast(BF16)
            self.stat = E(nc.sbuf_tensor("stat", [128, 96], F32))[:]
            self.tiny = E(nc.sbuf_tensor("tiny", [128, 16], F32))[:]
            self.rrv = E(nc.sbuf_tensor("rrv", [128, NCHUNK], F32))[:]
            self.onesr = E(nc.sbuf_tensor("onesr", [1, 128], F32))[:]
            self.arbuf = E(nc.sbuf_tensor("arbuf", [1, 16], F32))[:]
            self.kvst = E(nc.sbuf_tensor("kvst", [128, D], F32))[:]
            self.psum = [E(nc.psum_tensor(f"pb{i}", [128, 512], F32))[:] for i in range(8)]
            sems = {}
            for s in ("s_ld", "s_g", "s_q2", "s_t", "s_v", "s_a", "s_cc", "s_st"):
                sems[s] = E(nc.semaphore(s))
            self.block = E(nc.Block())
            self.ld = Trk(sems["s_ld"])
            self.gq = Trk(sems["s_g"])
            self.q2 = Trk(sems["s_q2"])
            self.tt = Trk(sems["s_t"])
            self.vv = Trk(sems["s_v"])
            self.aa = Trk(sems["s_a"])
            self.cc = Trk(sems["s_cc"])
            self.st = Trk(sems["s_st"])

            self.al.push()
            self.xh_sb = self.al.take(KC * TOK // 2, BF16)
            self.xl_sb = self.al.take(KC * TOK // 2, BF16)
            self.xh3 = self.xh_sb.rearrange("p (k t) -> p k t", k=KC)
            self.xl3 = self.xl_sb.rearrange("p (k t) -> p k t", k=KC)
            self.phase_L()
            if self.upto >= 2:
                self.phase_P()
            if self.upto >= 1:
                self.phase_W()
            self.al.pop()
            if self.upto >= 3:
                self.phase_A()
            if self.upto >= 4:
                self.phase_C()
            if self.upto >= 5:
                self.phase_F()
            else:
                self.stub_out()
            self.final_wait()
        return nc

    def phase_L(self):
        nc, al, block = self.nc, self.al, self.block
        ld, tt, vv, aa, gq = self.ld, self.tt, self.vv, self.aa, self.gq
        psum, stat, ident = self.psum, self.stat, self.ident

        al.push()
        xbuf = [al.take(D) for _ in range(2)]
        scr1 = al.take(D)
        scr2 = al.take(D)
        xn_i = al.take(D)
        t1b = al.take(D)
        xqf = al.take(D)
        xqi = al.take(D, I32)
        xqb = al.take(D // 2, BF16)
        self.wnorm_s = wnorm_s = al.take(D)
        xqT_sb = al.take(KC * TOK // 2, BF16)
        xh3, xl3 = self.xh3, self.xl3
        xq3 = xqT_sb.rearrange("p (k t) -> p k t", k=KC)

        @block.sync
        def _(sync):
            ld.inc(sync.dma_start(out=self.ident[:], in_=self.ident_d), 16)
            ld.inc(sync.dma_start(out=wnorm_s, in_=self.wnorm_d), 16)
            ld.inc(sync.dma_start(out=xbuf[0], in_=self.xin[0:128, :]), 16)
            ld.inc(sync.dma_start(out=xbuf[1], in_=self.xin[128:256, :]), 16)
        x1_ld = [ld.n - 32 + 16 * (i + 1) for i in range(2)]

        consumed = [0] * NTILE
        for i in range(NTILE):
            @block.vector
            def _(vector, i=i):
                vector.wait_ge(ld.sem, x1_ld[i])
                if i == 0:
                    vector.memset(self.onesr[:], 1.0)
                    vector.tensor_copy(out=self.ident_bf[:, 0:128], in_=ident[:])
                vv.inc(vector.tensor_scalar(out=scr1, in0=xbuf[i % 2], scalar1=1.0, scalar2=0.0,
                                            op0=ALU.mult, op1=ALU.add, accum_out=stat[:, 0 + i:1 + i]))

            @block.scalar
            def _(scalar, i=i):
                scalar.wait_ge(ld.sem, x1_ld[i])
                scalar.wait_ge(vv.sem, vv.n)
                aa.inc(scalar.activation(out=scr2, in_=xbuf[i % 2], func=AF.Square,
                                         accum_out=stat[:, 8 + i:9 + i]))
            consumed[i] = aa.n

            if i + 2 < NTILE:
                @block.sync
                def _(sync, i=i):
                    sync.wait_ge(aa.sem, consumed[i])
                    ld.inc(sync.dma_start(out=xbuf[i % 2], in_=self.xin[(i + 2) * 128:(i + 3) * 128, :]), 16)
                x1_ld.append(ld.n)

        @block.vector
        def _(vector):
            vector.wait_ge(aa.sem, aa.n)
            vector.drain()
            m = stat[:, 16:24]
            t_a, t_b = stat[:, 72:80], stat[:, 80:88]
            vector.tensor_scalar(out=m, in0=stat[:, 0:8], scalar1=INV_D, scalar2=None, op0=ALU.mult)
            vector.tensor_scalar(out=t_a, in0=stat[:, 8:16], scalar1=INV_D, scalar2=None, op0=ALU.mult)
            vector.drain()
            vector.tensor_tensor(out=t_b, in0=m, in1=m, op=ALU.mult)
            vector.drain()
            vector.tensor_tensor(out=t_b, in0=t_a, in1=t_b, op=ALU.subtract)
            vector.drain()
            vv.inc(vector.tensor_scalar(out=stat[:, 32:40], in0=t_b, scalar1=EPS, scalar2=None, op0=ALU.add))

        @block.scalar
        def _(scalar):
            scalar.wait_ge(vv.sem, vv.n)
            aa.inc(scalar.activation(out=stat[:, 88:96], in_=stat[:, 32:40], func=AF.Sqrt))

        @block.vector
        def _(vector):
            vector.wait_ge(aa.sem, aa.n)
            inv, var = stat[:, 24:32], stat[:, 80:88]
            vector.reciprocal(out=inv, in_=stat[:, 88:96])
            vector.drain()
            vector.tensor_tensor(out=stat[:, 40:48], in0=var, in1=inv, op=ALU.mult)
            vector.drain()
            vector.tensor_tensor(out=stat[:, 40:48], in0=stat[:, 40:48], in1=inv, op=ALU.mult)
            vector.drain()
            vv.inc(vector.tensor_scalar(out=stat[:, 40:48], in0=stat[:, 40:48],
                                        scalar1=EPS, scalar2=None, op0=ALU.add))

        @block.scalar
        def _(scalar):
            scalar.wait_ge(vv.sem, vv.n)
            aa.inc(scalar.activation(out=stat[:, 88:96], in_=stat[:, 40:48], func=AF.Sqrt))

        @block.vector
        def _(vector):
            vector.wait_ge(aa.sem, aa.n)
            vv.inc(vector.reciprocal(out=stat[:, 48:56], in_=stat[:, 88:96]))

        @block.sync
        def _(sync):
            sync.wait_ge(vv.sem, vv.n)
            ld.inc(sync.dma_start(out=xbuf[0], in_=self.xin[0:128, :]), 16)
            ld.inc(sync.dma_start(out=xbuf[1], in_=self.xin[128:256, :]), 16)
        x2_ld = [ld.n - 32 + 16 * (i + 1) for i in range(2)]

        xdone = [0] * NTILE
        for i in range(NTILE):
            ts = i * 128

            @block.vector
            def _(vector, i=i, ts=ts):
                vector.wait_ge(ld.sem, x2_ld[i])
                if i > 0:
                    vector.wait_ge(tt.sem, tposed[i - 1])
                s_m, s_inv = stat[:, 16 + i:17 + i], stat[:, 24 + i:25 + i]
                s_rf = stat[:, 48 + i:49 + i]
                s_A, s_is2 = stat[:, 56 + i:57 + i], stat[:, 64 + i:65 + i]
                t_d, t_e = stat[:, 88:89], stat[:, 89:90]
                vector.tensor_scalar(out=xn_i, in0=xbuf[i % 2], scalar1=s_m, scalar2=s_inv,
                                     op0=ALU.subtract, op1=ALU.mult)
                vector.drain()
                vector.tensor_tensor(out=t1b, in0=xn_i, in1=wnorm_s, op=ALU.mult)
                vector.drain()
                vector.tensor_scalar(out=scr1, in0=t1b, scalar1=1.0, scalar2=0.0,
                                     op0=ALU.mult, op1=ALU.max, accum_out=t_d)
                vector.tensor_scalar(out=scr1, in0=t1b, scalar1=-1.0, scalar2=0.0,
                                     op0=ALU.mult, op1=ALU.max, accum_out=t_e)
                vector.drain()
                vector.tensor_tensor(out=t_d, in0=t_d, in1=t_e, op=ALU.max)
                vector.drain()
                vector.tensor_tensor(out=t_d, in0=t_d, in1=s_rf, op=ALU.mult)
                vector.drain()
                vector.tensor_scalar(out=t_d, in0=t_d, scalar1=1e-5, scalar2=None, op0=ALU.max)
                vector.drain()
                vector.reciprocal(out=t_e, in_=t_d)
                vector.drain()
                vector.tensor_scalar(out=t_e, in0=t_e, scalar1=127.0, scalar2=None, op0=ALU.mult)
                vector.drain()
                vector.tensor_tensor(out=s_A, in0=s_rf, in1=t_e, op=ALU.mult)
                vector.tensor_scalar(out=s_is2, in0=t_d, scalar1=1.0 / 127.0, scalar2=None, op0=ALU.mult)
                vector.drain()
                vector.tensor_scalar(out=xqf, in0=t1b, scalar1=s_A, scalar2=None, op0=ALU.mult)
                vector.drain()
                vector.tensor_copy(out=xqi, in_=xqf)
                vector.drain()
                vv.inc(vector.tensor_copy(out=xqb, in_=xqi))
            xdone[i] = vv.n

            if i + 2 < NTILE:
                @block.sync
                def _(sync, i=i):
                    sync.wait_ge(vv.sem, xdone[i])
                    ld.inc(sync.dma_start(out=xbuf[i % 2], in_=self.xin[(i + 2) * 128:(i + 3) * 128, :]), 16)
                x2_ld.append(ld.n)

            if i == 0:
                tposed = [0] * NTILE

            @block.tensor
            def _(tensor, i=i):
                tensor.wait_ge(vv.sem, xdone[i])
                if i > 0:
                    tensor.wait_ge(vv.sem, evacd_v[i - 1])
                    tensor.wait_ge(aa.sem, evacd_a[i - 1])
                for b in range(4):
                    for j in range(4):
                        k = 4 * b + j
                        mm = tensor.transpose(psum[b][:, j * 128:(j + 1) * 128],
                                              xn_i[:, k * 128:(k + 1) * 128], ident)
                        if j == 3:
                            tt.inc(mm)
                for b in range(2):
                    for j in range(8):
                        k = 8 * b + j
                        mm = tensor.transpose(psum[4 + b][:, j * 64:(j + 1) * 64].bitcast(BF16),
                                              xqb[:, k * 128:(k + 1) * 128], self.ident_bf[:, 0:128])
                        if j == 7:
                            tt.inc(mm)
            tposed[i] = tt.n

            if i == 0:
                evacd_v, evacd_a = [0] * NTILE, [0] * NTILE

            @block.scalar
            def _(scalar, i=i, ts=ts):
                for b in range(4):
                    scalar.wait_ge(tt.sem, tposed[i] - 6 + b + 1)
                    aa.inc(scalar.activation(out=xh3[:, 4 * b:4 * b + 4, ts:ts + 128],
                                             in_=psum[b].rearrange("p (k t) -> p k t", k=4),
                                             func=AF.Copy))
            evacd_a[i] = aa.n

            @block.vector
            def _(vector, i=i, ts=ts):
                for b in range(4):
                    vector.wait_ge(aa.sem, evacd_a[i] - 4 + b + 1)
                    vv.inc(vector.tensor_tensor(out=xl3[:, 4 * b:4 * b + 4, ts:ts + 128],
                                                in0=psum[b].rearrange("p (k t) -> p k t", k=4),
                                                in1=xh3[:, 4 * b:4 * b + 4, ts:ts + 128],
                                                op=ALU.subtract))
                vector.wait_ge(tt.sem, tposed[i])
                vv.inc(vector.tensor_copy(out=xq3[:, 0:8, ts:ts + 128],
                                          in_=psum[4].bitcast(BF16).rearrange("p (k t) -> p k t", k=8)))
                vv.inc(vector.tensor_copy(out=xq3[:, 8:16, ts:ts + 128],
                                          in_=psum[5].bitcast(BF16).rearrange("p (k t) -> p k t", k=8)))
            evacd_v[i] = vv.n

        @block.gpsimd
        def _(gpsimd):
            gpsimd.wait_ge(vv.sem, vv.n)
            gq.inc(gpsimd.dma_start(out=self.xq_sp, in_=xqT_sb), 16)
        self.L_done_v = vv.n
        self.L_done_g = gq.n

        if "stats" in self.DBG:
            @block.gpsimd
            def _(gpsimd):
                gpsimd.wait_ge(vv.sem, self.L_done_v)
                gq.inc(gpsimd.dma_start(out=self.DBG["stats"], in_=stat), 16)
        if "xh" in self.DBG:
            cvt = scr1
            for nm, src in (("xh", self.xh_sb), ("xl", self.xl_sb), ("xq", xqT_sb)):
                if nm not in self.DBG:
                    continue
                for j in range(KC * TOK // 2048):
                    @block.vector
                    def _(vector, j=j, src=src):
                        if j == 0:
                            vector.wait_ge(vv.sem, self.L_done_v)
                        vector.wait_ge(gq.sem, gq.n)
                        vv.inc(vector.tensor_copy(out=cvt, in_=src[:, j * 2048:(j + 1) * 2048]))

                    @block.gpsimd
                    def _(gpsimd, j=j, nm=nm):
                        gpsimd.wait_ge(vv.sem, vv.n)
                        gq.inc(gpsimd.dma_start(out=self.DBG[nm][:, j * 2048:(j + 1) * 2048], in_=cvt), 16)
        al.pop()

    def phase_W(self):
        nc, al, block = self.nc, self.al, self.block
        ld, gq, tt, vv, aa, cc = self.ld, self.gq, self.tt, self.vv, self.aa, self.cc
        stat, tiny, psum = self.stat, self.tiny, self.psum

        al.push()
        shbuf = [al.take(2048) for _ in range(2)]
        posb = al.take(512, FP8)
        negb = al.take(512, FP8)
        terb = [al.take(512, FP8) for _ in range(2)]
        gsum = self.stat[:, 90:91]
        dsum = self.stat[:, 91:92]
        psum_col = self.stat[:, 92:93]
        pr2 = al.take(128)

        groups = [(g, list(range(4 * g, min(4 * g + 4, SH_CH)))) for g in range((SH_CH + 3) // 4)]

        @block.vector
        def _(vector):
            vector.memset(gsum, 0.0)
            vv.inc(vector.memset(dsum, 0.0))
        sums_ready = []
        for g, chs in groups:
            @block.sync
            def _(sync, g=g, chs=chs, tt_n=tt.n, aa_n=aa.n, gq_n=gq.n):
                if g == 0:
                    sync.wait_ge(gq.sem, gq_n)
                    sync.wait_ge(tt.sem, tt_n)
                    sync.wait_ge(aa.sem, aa_n)
                if g >= 2:
                    sync.wait_ge(aa.sem, sums_ready[g - 2])
                ld.inc(sync.dma_start(
                    out=shbuf[g % 2][:, 0:512 * len(chs)].rearrange("p (c e) -> p c e", c=len(chs)),
                    in_=self.wshard[chs[0]:chs[0] + len(chs)].rearrange("c p e -> p c e")), 16)

            @block.scalar
            def _(scalar, g=g, chs=chs):
                scalar.wait_ge(ld.sem, self.ld.n)
                n = len(chs)
                aa.inc(scalar.activation(out=shbuf[g % 2][:, 0:512 * n], in_=shbuf[g % 2][:, 0:512 * n],
                                         func=AF.Abs, accum_out=psum_col))
            sums_ready.append(aa.n)

            @block.vector
            def _(vector, g=g, chs=chs):
                vector.wait_ge(aa.sem, sums_ready[g])
                vector.drain()
                tgt = gsum if chs[0] < SH_GCH else dsum
                vv.inc(vector.tensor_tensor(out=tgt, in0=tgt, in1=psum_col, op=ALU.add))

        @block.gpsimd
        def _(gpsimd):
            gpsimd.wait_ge(vv.sem, vv.n)
            self.cc.inc(gpsimd.tensor_reduce(out=pr2[0:1, 0:2], in_=stat[:, 90:92], axis=AX.C, op=ALU.add))

        @block.vector
        def _(vector):
            vector.wait_ge(cc.sem, cc.n)
            vector.memset(self.arbuf[:], 0.0)
            vector.drain()
            vv.inc(vector.tensor_copy(out=self.arbuf[0:1, 8:10], in_=pr2[0:1, 0:2]))

        @block.gpsimd
        def _(gpsimd):
            gpsimd.wait_ge(vv.sem, vv.n)
            gq.inc(gpsimd.dma_start(out=self.ar_in, in_=self.arbuf), 16)
            gpsimd.wait_ge(gq.sem, gq.n)
            cc.inc(gpsimd.collective_compute("AllReduce", ALU.add,
                                             replica_groups=[list(range(NCORES))],
                                             ins=[self.ar_in.opt()], outs=[self.ar_out.opt()]))
            gpsimd.wait_ge(cc.sem, cc.n)
            gq.inc(gpsimd.dma_start(out=self.arbuf, in_=self.ar_out), 16)

        @block.vector
        def _(vector):
            vector.wait_ge(gq.sem, gq.n)
            a, b = self.arbuf[0:1, 0:1], self.arbuf[0:1, 1:2]
            vector.tensor_scalar(out=a, in0=self.arbuf[0:1, 8:9], scalar1=1.0 / GATE_ELEMS,
                                 scalar2=1e-5, op0=ALU.mult, op1=ALU.max)
            vector.tensor_scalar(out=b, in0=self.arbuf[0:1, 9:10], scalar1=1.0 / DOWN_ELEMS,
                                 scalar2=1e-5, op0=ALU.mult, op1=ALU.max)
            vector.drain()
            vector.reciprocal(out=self.arbuf[0:1, 2:3], in_=a)
            vv.inc(vector.reciprocal(out=self.arbuf[0:1, 3:4], in_=b))
        if "swv" in self.DBG:
            @block.gpsimd
            def _(gpsimd):
                gpsimd.wait_ge(vv.sem, vv.n)
                gq.inc(gpsimd.dma_start(out=self.DBG["swv"], in_=self.arbuf), 16)

        @block.tensor
        def _(tensor):
            tensor.wait_ge(vv.sem, vv.n)
            tensor.wait_ge(vv.sem, self.L_done_v)
            tensor.drain()
            for j, src in enumerate([self.arbuf[0:1, 2:3], self.arbuf[0:1, 3:4],
                                     self.arbuf[0:1, 0:1], self.arbuf[0:1, 1:2]]):
                tt.inc(tensor.matmul(psum[6][:, j:j + 1], self.onesr, src, start=True, stop=True))
            tensor.drain()

        @block.vector
        def _(vector):
            vector.wait_ge(tt.sem, tt.n)
            vv.inc(vector.tensor_copy(out=tiny[:, 0:4], in_=psum[6][:, 0:4]))
        self.scales_v = vv.n

        tern_done = []
        for g, chs in groups:
            n = len(chs)

            @block.sync
            def _(sync, g=g, chs=chs, n=n):
                sync.wait_ge(vv.sem, self.scales_v)
                if g >= 2:
                    sync.wait_ge(vv.sem, tern_done[g - 2])
                ld.inc(sync.dma_start(
                    out=shbuf[g % 2][:, 0:512 * n].rearrange("p (c e) -> p c e", c=n),
                    in_=self.wshard[chs[0]:chs[0] + n].rearrange("c p e -> p c e")), 16)

            @block.vector
            def _(vector, g=g, chs=chs, n=n):
                vector.wait_ge(ld.sem, self.ld.n)
                if g >= 2:
                    vector.wait_ge(gq.sem, tern_g[g - 2])
                sc = tiny[:, 0:1] if chs[0] < SH_GCH else tiny[:, 1:2]
                src = shbuf[g % 2][:, 0:512 * n]
                vector.tensor_scalar(out=posb[:, 0:2048 * n // 4], in0=src, scalar1=sc, scalar2=0.5,
                                     op0=ALU.mult, op1=ALU.is_gt)
                vector.tensor_scalar(out=negb[:, 0:2048 * n // 4], in0=src, scalar1=sc, scalar2=-0.5,
                                     op0=ALU.mult, op1=ALU.is_lt)
                vector.drain()
                vv.inc(vector.tensor_tensor(out=terb[g % 2][:, 0:2048 * n // 4],
                                            in0=posb[:, 0:2048 * n // 4],
                                            in1=negb[:, 0:2048 * n // 4], op=ALU.subtract))
                vector.drain()
            tern_done.append(vv.n)

            if g == 0:
                tern_g = []

            @block.gpsimd
            def _(gpsimd, g=g, chs=chs, n=n):
                gpsimd.wait_ge(vv.sem, tern_done[g])
                gq.inc(gpsimd.dma_start(
                    out=self.tern_in[chs[0]:chs[0] + n].rearrange("c p e -> p c e"),
                    in_=terb[g % 2][:, 0:2048 * n // 4].rearrange("p (c e) -> p c e", c=n)), 16)
            tern_g.append(gq.n)

        @block.gpsimd
        def _(gpsimd):
            gpsimd.wait_ge(gq.sem, gq.n)
            cc.inc(gpsimd.collective_compute("AllGather", ALU.bypass,
                                             replica_groups=[list(range(NCORES))],
                                             ins=[self.tern_in.opt()], outs=[self.tern_out.opt()]))
        self.tern_cc = cc.n

        if "tern" in self.DBG:
            cvt = al.take(2048)
            me = 0
            for ch in range(SH_CH):
                @block.sync
                def _(sync, ch=ch):
                    if ch == 0:
                        sync.wait_ge(cc.sem, self.tern_cc)
                    sync.wait_ge(vv.sem, vv.n)
                    ld.inc(sync.dma_start(out=cvt[:, 0:128].bitcast(FP8), in_=self.tern_in[ch]), 16)

                @block.vector
                def _(vector, ch=ch, gq_n=gq.n):
                    vector.wait_ge(ld.sem, self.ld.n)
                    vector.wait_ge(gq.sem, gq_n)
                    vv.inc(vector.tensor_copy(out=cvt[:, 512:1024], in_=cvt[:, 0:128].bitcast(FP8)))

                @block.gpsimd
                def _(gpsimd, ch=ch):
                    gpsimd.wait_ge(vv.sem, vv.n)
                    gq.inc(gpsimd.dma_start(out=self.DBG["tern"][ch], in_=cvt[:, 512:1024]), 16)
        al.pop()

    def phase_P(self):
        nc, al, block = self.nc, self.al, self.block
        ld, gq, tt, vv, aa = self.ld, self.gq, self.tt, self.vv, self.aa
        psum = self.psum
        xh3, xl3 = self.xh3, self.xl3

        al.push()
        wblk = [al.take(512) for _ in range(2)]
        wbh = [al.take(256, BF16) for _ in range(2)]
        wbl = [al.take(256, BF16) for _ in range(2)]
        bounce = [al.take(512) for _ in range(2)]

        prev_evac_a = aa.n
        for wname in ("k", "v", "q"):
            wt = self.w_t[wname]
            sp = self.qkv_sp[wname]
            for mp in range(4):
                split_done = []
                for k in range(KC):
                    @block.sync
                    def _(sync, mp=mp, k=k, wt=wt, vv_emit=vv.n, first=(prev_evac_a == self.aa.n and mp == 0 and k == 0)):
                        if k >= 2:
                            sync.wait_ge(vv.sem, split_done[k - 2])
                        else:
                            sync.wait_ge(vv.sem, vv_emit)
                        if first:
                            sync.wait_ge(self.gq.sem, self.L_done_g)
                        ld.inc(sync.dma_start(out=wblk[k % 2], in_=wt[mp, k]), 16)

                    tt_at_emit = tt.n

                    @block.vector
                    def _(vector, k=k, tt_at_emit=tt_at_emit):
                        vector.wait_ge(ld.sem, self.ld.n)
                        if k >= 2:
                            vector.wait_ge(tt.sem, mm_done[k - 2])
                        else:
                            vector.wait_ge(tt.sem, tt_at_emit)
                        vector.tensor_copy(out=wbh[k % 2][:, 0:512], in_=wblk[k % 2])
                        vector.drain()
                        vv.inc(vector.tensor_tensor(out=wbl[k % 2][:, 0:512], in0=wblk[k % 2],
                                                    in1=wbh[k % 2][:, 0:512], op=ALU.subtract))
                    split_done.append(vv.n)

                    if k == 0:
                        mm_done = []

                    @block.tensor
                    def _(tensor, k=k, mp=mp):
                        tensor.wait_ge(vv.sem, split_done[k])
                        if k == 0:
                            tensor.wait_ge(aa.sem, prev_evac_a)
                        h = wbh[k % 2][:, 0:512]
                        l = wbl[k % 2][:, 0:512]
                        for t in range(8):
                            xh_k = xh3[:, k, t * 128:(t + 1) * 128]
                            xl_k = xl3[:, k, t * 128:(t + 1) * 128]
                            tensor.matmul(psum[t], xh_k, h, start=(k == 0), stop=False)
                            tensor.matmul(psum[t], xh_k, l, start=False, stop=False)
                            mm = tensor.matmul(psum[t], xl_k, h, start=False, stop=(k == KC - 1))
                            if t == 7:
                                tt.inc(mm)
                    mm_done.append(tt.n)

                for t in range(8):
                    @block.scalar
                    def _(scalar, t=t, mp=mp):
                        scalar.wait_ge(tt.sem, mm_done[KC - 1])
                        if t >= 2:
                            scalar.wait_ge(gq.sem, spilled[t - 2])
                        aa.inc(scalar.activation(out=bounce[t % 2], in_=psum[t], func=AF.Copy))

                    if t == 0:
                        spilled = []

                    @block.gpsimd
                    def _(gpsimd, t=t, mp=mp, sp=sp):
                        gpsimd.wait_ge(aa.sem, self.aa.n)
                        gq.inc(gpsimd.dma_start(out=sp[t * 128:(t + 1) * 128, mp * 512:(mp + 1) * 512],
                                                in_=bounce[t % 2]), 16)
                    spilled.append(gq.n)
                prev_evac_a = aa.n

        self.P_done_g = gq.n
        if "qkv" in self.DBG:
            cvt = al.take(2048)
            for j, nm in enumerate(("q", "k", "v")):
                for r in range(NTILE):
                    @block.sync
                    def _(sync, j=j, nm=nm, r=r):
                        sync.wait_ge(gq.sem, self.P_done_g)
                        sync.wait_ge(gq.sem, self.gq.n)
                        ld.inc(sync.dma_start(out=cvt, in_=self.qkv_sp[nm][r * 128:(r + 1) * 128, :]), 16)

                    @block.gpsimd
                    def _(gpsimd, j=j, r=r):
                        gpsimd.wait_ge(ld.sem, self.ld.n)
                        gq.inc(gpsimd.dma_start(out=self.DBG["qkv"][j, r * 128:(r + 1) * 128, :], in_=cvt), 16)
        al.pop()

    def phase_A(self):
        nc, al, block = self.nc, self.al, self.block
        ld, gq, tt, vv, aa, cc = self.ld, self.gq, self.tt, self.vv, self.aa, self.cc
        psum, ident, kvst = self.psum, self.ident, self.kvst
        rC = self.rc["rC"]

        lvl = 5
        for n in (1, 2, 3, 4):
            if f"a{n}" in self.dbg:
                lvl = n
        al.push()
        dmask_s = al.take(D)
        qdec_s = al.take(D)
        kdecb_s = al.take(D)
        cosb = [al.take(512) for _ in range(2)]
        sinb = [al.take(512) for _ in range(2)]
        qc = [al.take(D) for _ in range(2)]
        kc = [al.take(D) for _ in range(2)]
        vc = [al.take(D) for _ in range(2)]
        kd = al.take(D)
        rtmp = [al.take(512) for _ in range(4)]
        qT4 = al.take(512)
        kT4 = al.take(512)
        sm4 = al.take(512)
        qdT_c = [al.take(D) for _ in range(2)]
        attnT_c = [al.take(D) for _ in range(2)]

        @block.sync
        def _(sync, gq_n=gq.n, vv_n=vv.n):
            sync.wait_ge(gq.sem, gq_n)
            sync.wait_ge(vv.sem, vv_n)
            ld.inc(sync.dma_start(out=dmask_s.rearrange("p (g e) -> p g e", g=4), in_=self.dmask4_d.rearrange("g p e -> p g e")), 16)
            ld.inc(sync.dma_start(out=qdec_s.rearrange("p (g e) -> p g e", g=4), in_=self.qdec4_d.rearrange("g p e -> p g e")), 16)
            ld.inc(sync.dma_start(out=kdecb_s, in_=self.kdecb_d), 16)

        @block.vector
        def _(vector):
            vector.memset(kvst[:], 0.0)
            vv.inc(vector.memset(qT4, 0.0))
            vector.drain()
        kv_zero_v = vv.n

        chunk_ld = []
        rot_done = [0] * NCHUNK
        spill_g = [0] * NCHUNK
        evQ = evK = evS = evO = evU = 0
        kvup = [[0] * 4 for _ in range(NCHUNK)]

        for c in range(NCHUNK):
            @block.sync
            def _(sync, c=c, rd=rot_done, tt_n=tt.n, vv_n=vv.n, gq_n=gq.n):
                if c < 2:
                    sync.wait_ge(vv.sem, vv_n)
                    sync.wait_ge(gq.sem, gq_n)
                if c >= 2:
                    sync.wait_ge(tt.sem, grp_tt[c - 2])
                for nm, buf in (("q", qc[c % 2]), ("k", kc[c % 2]), ("v", vc[c % 2])):
                    ld.inc(sync.dma_start(out=buf, in_=self.qkv_sp[nm][c * 128:(c + 1) * 128, :]), 16)
                ld.inc(sync.dma_start(out=cosb[c % 2], in_=self.cosrep_d[c]), 16)
                ld.inc(sync.dma_start(out=sinb[c % 2], in_=self.sinrep_d[c]), 16)
            chunk_ld.append(ld.n)

            @block.vector
            def _(vector, c=c, prev_tt=(grp_tt[c - 1] if c > 0 else 0)):
                vector.wait_ge(ld.sem, chunk_ld[c])
                if c == 0:
                    vector.wait_ge(vv.sem, kv_zero_v)
                else:
                    vector.wait_ge(tt.sem, prev_tt)
                cs = cosb[c % 2].rearrange("p (h j) -> p h j", h=H)
                sn = sinb[c % 2].rearrange("p (h j) -> p h j", h=H)
                for buf in (qc[c % 2], kc[c % 2]):
                    b3 = buf.rearrange("p (h d) -> p h d", h=H)
                    x1 = b3[:, :, 0:32]
                    x2 = b3[:, :, 64:96]
                    r0 = rtmp[0].rearrange("p (h j) -> p h j", h=H)
                    r1 = rtmp[1].rearrange("p (h j) -> p h j", h=H)
                    r2 = rtmp[2].rearrange("p (h j) -> p h j", h=H)
                    r3 = rtmp[3].rearrange("p (h j) -> p h j", h=H)
                    vector.tensor_tensor(out=r0, in0=x1, in1=cs, op=ALU.mult)
                    vector.tensor_tensor(out=r1, in0=x2, in1=sn, op=ALU.mult)
                    vector.tensor_tensor(out=r2, in0=x1, in1=sn, op=ALU.mult)
                    vector.tensor_tensor(out=r3, in0=x2, in1=cs, op=ALU.mult)
                    vector.drain()
                    vector.tensor_tensor(out=x1, in0=r0, in1=r1, op=ALU.add)
                    vector.tensor_tensor(out=x2, in0=r3, in1=r2, op=ALU.subtract)
                    vector.drain()
                vv.inc(vector.tensor_tensor(out=kd, in0=kc[c % 2], in1=kdecb_s, op=ALU.mult))
            rot_done[c] = vv.n

            grp_tt = grp_tt if c > 0 else [0] * NCHUNK
            if lvl < 2:
                grp_tt[c] = tt.n
                continue
            for g in range(4):
                @block.tensor
                def _(tensor, c=c, g=g, evQ=evQ, evK=evK):
                    tensor.wait_ge(vv.sem, rot_done[c])
                    if evQ:
                        tensor.wait_ge(aa.sem, evK)
                        tensor.wait_ge(vv.sem, evQ)
                    for hq in range(4):
                        h = 4 * g + hq
                        mm = tensor.transpose(psum[0][:, hq * 128:(hq + 1) * 128],
                                              qc[c % 2][:, h * 128:(h + 1) * 128], ident)
                        if hq == 3:
                            tt.inc(mm)
                    for hq in range(4):
                        h = 4 * g + hq
                        mm = tensor.transpose(psum[1][:, hq * 128:(hq + 1) * 128],
                                              kc[c % 2][:, h * 128:(h + 1) * 128], ident)
                        if hq == 3:
                            tt.inc(mm)
                tposeQ, tposeK = tt.n - 1, tt.n

                @block.scalar
                def _(scalar, c=c, g=g, tposeQ=tposeQ, tposeK=tposeK):
                    scalar.wait_ge(tt.sem, tposeQ)
                    aa.inc(scalar.activation(out=qT4, in_=psum[0], func=AF.Copy))
                    scalar.wait_ge(tt.sem, tposeK)
                    aa.inc(scalar.activation(out=kT4, in_=psum[1], func=AF.Copy))
                evK = aa.n

                @block.vector
                def _(vector, c=c, g=g, qa=aa.n - 1, spv=spill_g[c - 2] if c >= 2 else 0):
                    vector.wait_ge(aa.sem, qa)
                    if g == 0 and spv:
                        vector.wait_ge(gq.sem, spv)
                    vv.inc(vector.tensor_tensor(out=qdT_c[c % 2][:, g * 512:(g + 1) * 512],
                                                in0=qT4, in1=qdec_s[:, g * 512:(g + 1) * 512],
                                                op=ALU.mult))
                evQ = vv.n

                if lvl < 3:
                    grp_tt[c] = tt.n
                    continue

                @block.tensor
                def _(tensor, c=c, g=g, evK=evK, evQ=evQ, evS=evS):
                    tensor.wait_ge(aa.sem, evK)
                    tensor.wait_ge(vv.sem, evQ)
                    if evS:
                        tensor.wait_ge(vv.sem, evS)
                    for hq in range(4):
                        mm = tensor.matmul(psum[2][:, hq * 128:(hq + 1) * 128],
                                           kT4[:, hq * 128:(hq + 1) * 128],
                                           qT4[:, hq * 128:(hq + 1) * 128],
                                           start=True, stop=True)
                        if hq == 3:
                            tt.inc(mm)
                sc_tt = tt.n

                @block.vector
                def _(vector, c=c, g=g, sc_tt=sc_tt):
                    vector.wait_ge(tt.sem, sc_tt)
                    vv.inc(vector.tensor_tensor(out=sm4, in0=psum[2],
                                                in1=dmask_s[:, g * 512:(g + 1) * 512], op=ALU.mult))
                evS = vv.n

                if lvl < 4:
                    grp_tt[c] = tt.n
                    kvup[c][g] = vv.n
                    continue

                @block.tensor
                def _(tensor, c=c, g=g, evS=evS, evO=evO, evU=evU,
                      kv_prev=(kvup[c - 1][g] if c > 0 else kv_zero_v)):
                    tensor.wait_ge(vv.sem, evS)
                    tensor.wait_ge(vv.sem, kv_prev)
                    if evO:
                        tensor.wait_ge(aa.sem, evO)
                    if evU:
                        tensor.wait_ge(vv.sem, evU)
                    for hq in range(4):
                        h = 4 * g + hq
                        tensor.matmul(psum[3][:, hq * 128:(hq + 1) * 128],
                                      vc[c % 2][:, h * 128:(h + 1) * 128],
                                      sm4[:, hq * 128:(hq + 1) * 128], start=True, stop=False)
                        mm = tensor.matmul(psum[3][:, hq * 128:(hq + 1) * 128],
                                           kvst[:, h * 128:(h + 1) * 128],
                                           qdT_c[c % 2][:, (g * 512 + hq * 128):(g * 512 + (hq + 1) * 128)],
                                           start=False, stop=True, skip_group_check=True)
                        if hq == 3:
                            tt.inc(mm)
                    for hq in range(4):
                        h = 4 * g + hq
                        mm = tensor.matmul(psum[4][:, hq * 128:(hq + 1) * 128],
                                           kd[:, h * 128:(h + 1) * 128],
                                           vc[c % 2][:, h * 128:(h + 1) * 128], start=True, stop=True)
                        if hq == 3:
                            tt.inc(mm)
                out_tt, u_tt = tt.n - 1, tt.n
                grp_tt[c] = tt.n

                @block.scalar
                def _(scalar, c=c, g=g, out_tt=out_tt, spa=(spill_g[c - 2] if c >= 2 else 0)):
                    scalar.wait_ge(tt.sem, out_tt)
                    if g == 0 and spa:
                        scalar.wait_ge(gq.sem, spa)
                    aa.inc(scalar.activation(out=attnT_c[c % 2][:, g * 512:(g + 1) * 512],
                                             in_=psum[3], func=AF.Copy))
                evO = aa.n

                if lvl < 5:
                    kvup[c][g] = vv.n
                    continue

                @block.vector
                def _(vector, c=c, g=g, u_tt=u_tt):
                    vector.wait_ge(tt.sem, u_tt)
                    for hq in range(4):
                        h = 4 * g + hq
                        mm = vector.scalar_tensor_tensor(
                            out=kvst[:, h * 128:(h + 1) * 128],
                            in0=kvst[:, h * 128:(h + 1) * 128],
                            scalar=float(rC[h]),
                            in1=psum[4][:, hq * 128:(hq + 1) * 128],
                            op0=ALU.mult, op1=ALU.add)
                        if hq == 3:
                            vv.inc(mm)
                    vector.drain()
                evU = vv.n
                kvup[c][g] = vv.n

            if lvl >= 4:
                @block.gpsimd
                def _(gpsimd, c=c, aan=aa.n, vvn=vv.n):
                    gpsimd.wait_ge(aa.sem, aan)
                    gpsimd.wait_ge(vv.sem, vvn)
                    gq.inc(gpsimd.dma_start(out=self.attn_sp[c], in_=attnT_c[c % 2]), 16)
                    gq.inc(gpsimd.dma_start(out=self.qd_sp[c], in_=qdT_c[c % 2]), 16)
            spill_g[c] = gq.n

        if lvl < 5 or "noz" in self.dbg:
            self.z_cc = cc.n
            if "attn" in self.DBG and lvl >= 4:
                for c in range(NCHUNK):
                    @block.sync
                    def _(sync, c=c, gqn=gq.n):
                        sync.wait_ge(gq.sem, gqn)
                        ld.inc(sync.dma_start(out=qc[0], in_=self.attn_sp[c]), 16)

                    @block.gpsimd
                    def _(gpsimd, c=c, ldn=ld.n):
                        gpsimd.wait_ge(ld.sem, ldn)
                        gq.inc(gpsimd.dma_start(out=self.DBG["attn"][c], in_=qc[0]), 16)
            al.pop()
            return

        @block.gpsimd
        def _(gpsimd, vvn=vv.n):
            gpsimd.wait_ge(vv.sem, vvn)
            gq.inc(gpsimd.dma_start(out=self.z_in, in_=kvst[:]), 16)
            gpsimd.wait_ge(gq.sem, gq.n)
            cc.inc(gpsimd.collective_compute("AllGather", ALU.bypass,
                                             replica_groups=[list(range(NCORES))],
                                             ins=[self.z_in.opt()], outs=[self.z_out.opt()]))
        self.z_cc = cc.n

        if "attn" in self.DBG:
            for c in range(NCHUNK):
                @block.sync
                def _(sync, c=c, gqn=gq.n, zcc=self.z_cc):
                    sync.wait_ge(gq.sem, gqn)
                    sync.wait_ge(cc.sem, zcc)
                    ld.inc(sync.dma_start(out=qc[0], in_=self.attn_sp[c]), 16)

                @block.gpsimd
                def _(gpsimd, c=c, ldn=ld.n):
                    gpsimd.wait_ge(ld.sem, ldn)
                    gq.inc(gpsimd.dma_start(out=self.DBG["attn"][c], in_=qc[0]), 16)
        if "z" in self.DBG:
            @block.gpsimd
            def _(gpsimd, ccn=cc.n):
                gpsimd.wait_ge(cc.sem, ccn)
                for r in range(NCORES):
                    gq.inc(gpsimd.dma_start(out=self.DBG["z"][r * 128:(r + 1) * 128, :],
                                            in_=self.z_out[r * 128:(r + 1) * 128, :]), 16)
        al.pop()

    def phase_C(self):
        nc, al, block = self.nc, self.al, self.block
        ld, gq, tt, vv, aa, cc = self.ld, self.gq, self.tt, self.vv, self.aa, self.cc
        psum, ident, kvst, stat = self.psum, self.ident, self.kvst, self.stat
        rc_pow = self.rc["rc_pow"]

        al.push()
        zbuf = [al.take(D) for _ in range(2)]
        zw_s = al.take(128)
        at_b = [al.take(D) for _ in range(2)]
        qd_b = [al.take(D) for _ in range(2)]
        qs4 = al.take(512)
        sqt = al.take(D)
        rowst = al.take(2048)
        onescol = al.take(128)
        ah_sb = al.take(NCHUNK * D // 2, BF16)
        al_sb = al.take(NCHUNK * D // 2, BF16)
        self.py_done_g = 0

        @block.sync
        def _(sync, ccn=self.z_cc):
            sync.wait_ge(cc.sem, ccn)
            ld.inc(sync.dma_start(out=zw_s[:, 0:128], in_=self.zw_d), 16)
            ld.inc(sync.dma_start(out=zbuf[0], in_=self.z_out[0:128, :]), 16)
            ld.inc(sync.dma_start(out=zbuf[1], in_=self.z_out[128:256, :]), 16)

        @block.vector
        def _(vector, ldn=ld.n):
            vector.wait_ge(ld.sem, ldn - 32)
            vector.memset(kvst[:], 0.0)
            vv.inc(vector.drain() if False else vector.memset(rowst, 0.0))
            vector.drain()
        zl = [ld.n - 16, ld.n]
        zdone = []
        for cp in range(NCORES):
            @block.vector
            def _(vector, cp=cp, zl0=zl[cp]):
                vector.wait_ge(ld.sem, zl0)
                for h in range(H):
                    mm = vector.scalar_tensor_tensor(
                        out=kvst[:, h * 128:(h + 1) * 128],
                        in0=zbuf[cp % 2][:, h * 128:(h + 1) * 128],
                        scalar=zw_s[:, cp * 16 + h:cp * 16 + h + 1],
                        in1=kvst[:, h * 128:(h + 1) * 128],
                        op0=ALU.mult, op1=ALU.add)
                    if h == H - 1:
                        vv.inc(mm)
                vector.drain()
            zdone.append(vv.n)
            if cp + 2 < NCORES:
                @block.sync
                def _(sync, cp=cp):
                    sync.wait_ge(vv.sem, zdone[cp])
                    ld.inc(sync.dma_start(out=zbuf[cp % 2], in_=self.z_out[(cp + 2) * 128:(cp + 3) * 128, :]), 16)
                zl.append(ld.n)
        sstart_v = vv.n

        atl = []
        rms_a = []
        rms_sq = []
        for c in range(NCHUNK):
            @block.sync
            def _(sync, c=c, prev=(atl[c - 2] if c >= 2 else 0),
                  prev_sq=(rms_sq[c - 2] if c >= 2 else 0)):
                if c >= 2:
                    sync.wait_ge(vv.sem, prev)
                    sync.wait_ge(aa.sem, prev_sq)
                ld.inc(sync.dma_start(out=qd_b[c % 2], in_=self.qd_sp[c]), 16)
                ld.inc(sync.dma_start(out=at_b[c % 2], in_=self.attn_sp[c]), 16)
            this_ld = ld.n

            corr_tt = []
            for g in range(4):
                @block.vector
                def _(vector, c=c, g=g, this_ld=this_ld, ctt=(corr_tt[g - 1] if g else 0)):
                    vector.wait_ge(ld.sem, this_ld)
                    if g > 0:
                        vector.wait_ge(tt.sem, ctt)
                    for hq in range(4):
                        h = 4 * g + hq
                        mm = vector.tensor_scalar(
                            out=qs4[:, hq * 128:(hq + 1) * 128],
                            in0=qd_b[c % 2][:, (g * 512 + hq * 128):(g * 512 + (hq + 1) * 128)],
                            scalar1=float(rc_pow[h][c]), scalar2=None, op0=ALU.mult)
                        if hq == 3:
                            vv.inc(mm)
                qs_v = vv.n

                @block.tensor
                def _(tensor, c=c, g=g, qs_v=qs_v, sstart_v=sstart_v, pv=0):
                    tensor.wait_ge(vv.sem, qs_v)
                    tensor.wait_ge(vv.sem, sstart_v)
                    for hq in range(4):
                        h = 4 * g + hq
                        mm = tensor.matmul(psum[5][:, hq * 128:(hq + 1) * 128],
                                           kvst[:, h * 128:(h + 1) * 128],
                                           qs4[:, hq * 128:(hq + 1) * 128], start=True, stop=True)
                        if hq == 3:
                            tt.inc(mm)
                corr_tt.append(tt.n)

                @block.vector
                def _(vector, c=c, g=g, ctt=corr_tt[g]):
                    vector.wait_ge(tt.sem, ctt)
                    vv.inc(vector.tensor_tensor(out=at_b[c % 2][:, g * 512:(g + 1) * 512],
                                                in0=at_b[c % 2][:, g * 512:(g + 1) * 512],
                                                in1=psum[5], op=ALU.add))
                    vector.drain()
            corr_v = vv.n

            @block.scalar
            def _(scalar, c=c, corr_v=corr_v):
                scalar.wait_ge(vv.sem, corr_v)
                aa.inc(scalar.activation(out=sqt, in_=at_b[c % 2], func=AF.Square))
            sq_a = aa.n
            rms_sq.append(sq_a)

            @block.vector
            def _(vector, c=c):
                if c == 0:
                    vector.memset(onescol[:, 0:1], 1.0)
                    vv.inc(vector.drain() if False else vector.memset(rowst[0:1, 1900:1901], 0.0))
                    vector.drain()
            ones_v = vv.n

            @block.tensor
            def _(tensor, c=c, sq_a=sq_a, ones_v=ones_v, prev_red=(rms_a[c - 1] if c else 0)):
                tensor.wait_ge(aa.sem, sq_a)
                tensor.wait_ge(vv.sem, ones_v)
                if prev_red:
                    tensor.wait_ge(vv.sem, prev_red)
                for j in range(4):
                    mm = tensor.matmul(psum[4 + j][0:1, 0:512], onescol[:, 0:1],
                                       sqt[:, j * 512:(j + 1) * 512], start=True, stop=True)
                    if j == 3:
                        tt.inc(mm)
            red_tt = tt.n

            @block.vector
            def _(vector, c=c, red_tt=red_tt):
                vector.wait_ge(tt.sem, red_tt)
                for j in range(4):
                    vector.tensor_reduce(out=rowst[0:1, 1024 + j * 128:1024 + (j + 1) * 128],
                                         in_=psum[4 + j][0:1, :].rearrange("p (h t) -> p t h", h=4),
                                         axis=AX.X, op=ALU.add)
                vector.drain()
                vector.tensor_tensor(out=rowst[0:1, 1024:1152], in0=rowst[0:1, 1024:1152],
                                     in1=rowst[0:1, 1152:1280], op=ALU.add)
                vector.tensor_tensor(out=rowst[0:1, 1280:1408], in0=rowst[0:1, 1280:1408],
                                     in1=rowst[0:1, 1408:1536], op=ALU.add)
                vector.drain()
                vector.tensor_tensor(out=rowst[0:1, 1024:1152], in0=rowst[0:1, 1024:1152],
                                     in1=rowst[0:1, 1280:1408], op=ALU.add)
                vector.drain()
                vv.inc(vector.tensor_scalar(out=rowst[0:1, c * 128:(c + 1) * 128],
                                            in0=rowst[0:1, 1024:1152],
                                            scalar1=INV_D, scalar2=EPS, op0=ALU.mult, op1=ALU.add))
                vector.drain()
            rms_a.append(vv.n)

            @block.vector
            def _(vector, c=c):
                ah = ah_sb.rearrange("p (c e) -> p c e", c=NCHUNK)[:, c, :]
                alo = al_sb.rearrange("p (c e) -> p c e", c=NCHUNK)[:, c, :]
                vector.tensor_copy(out=ah, in_=at_b[c % 2])
                vector.drain()
                vv.inc(vector.tensor_tensor(out=alo, in0=at_b[c % 2], in1=ah, op=ALU.subtract))
                vector.drain()
            atl.append(vv.n)

            if "attnc" in self.DBG:
                @block.gpsimd
                def _(gpsimd, c=c, vvn=vv.n):
                    gpsimd.wait_ge(vv.sem, vvn)
                    gq.inc(gpsimd.dma_start(out=self.DBG["attnc"][c], in_=at_b[c % 2]), 16)

        @block.scalar
        def _(scalar, rv=rms_a[-1]):
            scalar.wait_ge(vv.sem, rv)
            aa.inc(scalar.activation(out=rowst[0:1, 1024:1024 + 1024], in_=rowst[0:1, 0:1024], func=AF.Sqrt))
        sq2_a = aa.n

        @block.vector
        def _(vector, sq2_a=sq2_a):
            vector.wait_ge(aa.sem, sq2_a)
            vv.inc(vector.reciprocal(out=rowst[0:1, 0:1024], in_=rowst[0:1, 1024:2048]))
            vector.drain()
        rr_v = vv.n

        @block.tensor
        def _(tensor, rr_v=rr_v):
            tensor.wait_ge(vv.sem, rr_v)
            tensor.drain()
            for c in range(NCHUNK):
                mm = tensor.matmul(psum[7][:, c:c + 1], rowst[0:1, c * 128:(c + 1) * 128],
                                   onescol[0:1, 0:1], start=True, stop=True, skip_group_check=True)
                if c == NCHUNK - 1:
                    tt.inc(mm)
            tensor.drain()
        rrT_tt = tt.n

        @block.vector
        def _(vector, rrT_tt=rrT_tt):
            vector.wait_ge(tt.sem, rrT_tt)
            vv.inc(vector.tensor_copy(out=self.rrv[:, 0:NCHUNK], in_=psum[7][:, 0:NCHUNK]))
        rr_ready = vv.n
        if "rr" in self.DBG:
            @block.gpsimd
            def _(gpsimd, rr_ready=rr_ready):
                gpsimd.wait_ge(vv.sem, rr_ready)
                gq.inc(gpsimd.dma_start(out=self.DBG["rr"][:, 0:NCHUNK], in_=self.rrv[:, 0:NCHUNK]), 16)
                gq.inc(gpsimd.dma_start(out=self.DBG["rrow"], in_=rowst[0:1, 0:2048]), 16)

        self.py_sp = nc.dram_tensor("py_sp", [TOK, D], F32).ap()
        wblk = [al.take(512) for _ in range(2)]
        wbh = [al.take(256, BF16) for _ in range(2)]
        wbl = [al.take(256, BF16) for _ in range(2)]
        bounce = [al.take(512) for _ in range(2)]
        xb2 = [al.take(D) for _ in range(2)]
        rms_s = al.take(KC)

        @block.sync
        def _(sync, rr_ready=rr_ready):
            sync.wait_ge(vv.sem, rr_ready)
            ld.inc(sync.dma_start(out=rms_s[:, 0:KC], in_=self.rms_pp_d), 16)
            ld.inc(sync.dma_start(out=xb2[0], in_=self.xin[0:128, :]), 16)
            ld.inc(sync.dma_start(out=xb2[1], in_=self.xin[128:256, :]), 16)
        x_ld = [ld.n - 16, ld.n]

        prev_evac_v = 0
        ah3 = ah_sb.rearrange("p (c e) -> p c e", c=NCHUNK)
        al3 = al_sb.rearrange("p (c e) -> p c e", c=NCHUNK)
        spilled_py = []
        for mp in range(4):
            split_done = []
            mm_done = []
            for k in range(KC):
                @block.sync
                def _(sync, mp=mp, k=k, vv_emit=vv.n):
                    if k >= 2:
                        sync.wait_ge(vv.sem, split_done[k - 2])
                    else:
                        sync.wait_ge(vv.sem, vv_emit)
                    ld.inc(sync.dma_start(out=wblk[k % 2], in_=self.w_t["o"][mp, k]), 16)

                @block.vector
                def _(vector, k=k, tt_at_emit=tt.n, ldn=ld.n):
                    vector.wait_ge(ld.sem, ldn)
                    if k >= 2:
                        vector.wait_ge(tt.sem, mm_done[k - 2])
                    else:
                        vector.wait_ge(tt.sem, tt_at_emit)
                    vector.tensor_scalar(out=wblk[k % 2], in0=wblk[k % 2],
                                         scalar1=rms_s[:, k:k + 1], scalar2=None, op0=ALU.mult)
                    vector.drain()
                    vector.tensor_copy(out=wbh[k % 2][:, 0:512], in_=wblk[k % 2])
                    vector.drain()
                    vv.inc(vector.tensor_tensor(out=wbl[k % 2][:, 0:512], in0=wblk[k % 2],
                                                in1=wbh[k % 2][:, 0:512], op=ALU.subtract))
                split_done.append(vv.n)

                @block.tensor
                def _(tensor, mp=mp, k=k, sd=split_done[k], pe=prev_evac_v, atv=rr_ready):
                    tensor.wait_ge(vv.sem, sd)
                    if k == 0:
                        tensor.wait_ge(vv.sem, atv)
                        if pe:
                            tensor.wait_ge(vv.sem, pe)
                    h = wbh[k % 2][:, 0:512]
                    l = wbl[k % 2][:, 0:512]
                    for t in range(8):
                        ah_k = ah3[:, t, k * 128:(k + 1) * 128]
                        al_k = al3[:, t, k * 128:(k + 1) * 128]
                        tensor.matmul(psum[t], ah_k, h, start=(k == 0), stop=False)
                        tensor.matmul(psum[t], ah_k, l, start=False, stop=False)
                        mm = tensor.matmul(psum[t], al_k, h, start=False, stop=(k == KC - 1))
                        if t == 7:
                            tt.inc(mm)
                mm_done.append(tt.n)

            for t in range(8):
                @block.scalar
                def _(scalar, t=t, mp=mp, sp=(spilled_py[-2] if len(spilled_py) >= 2 else 0)):
                    if sp:
                        scalar.wait_ge(gq.sem, sp)
                    self.q2.inc(scalar.dma_start(out=xb2[t % 2],
                                                 in_=self.xin[t * 128:(t + 1) * 128, :]), 16)

                @block.vector
                def _(vector, t=t, mp=mp, mmd=mm_done[KC - 1], ldn=self.q2.n,
                      spg=(spilled_py[-2] if len(spilled_py) >= 2 else 0)):
                    vector.wait_ge(tt.sem, mmd)
                    vector.wait_ge(self.q2.sem, ldn)
                    if spg:
                        vector.wait_ge(gq.sem, spg)
                    if "xbdump" in self.dbg:
                        vector.tensor_scalar(out=bounce[t % 2],
                                             in0=xb2[t % 2][:, mp * 512:(mp + 1) * 512],
                                             scalar1=1.0, scalar2=None, op0=ALU.mult)
                    else:
                        vector.tensor_scalar(out=bounce[t % 2], in0=psum[t],
                                             scalar1=self.rrv[:, t:t + 1], scalar2=None,
                                             op0=ALU.mult)
                    if "nox" in self.dbg:
                        vv.inc(vector.drain() if False else vector.tensor_scalar(
                            out=bounce[t % 2], in0=bounce[t % 2], scalar1=1.0, scalar2=None, op0=ALU.mult))
                    else:
                        vector.drain()
                        vv.inc(vector.tensor_tensor(out=bounce[t % 2], in0=bounce[t % 2],
                                                    in1=xb2[t % 2][:, mp * 512:(mp + 1) * 512], op=ALU.add))
                ev_v = vv.n

                @block.gpsimd
                def _(gpsimd, t=t, mp=mp, ev_v=ev_v):
                    gpsimd.wait_ge(vv.sem, ev_v)
                    gq.inc(gpsimd.dma_start(out=self.py_sp[t * 128:(t + 1) * 128, mp * 512:(mp + 1) * 512],
                                            in_=bounce[t % 2]), 16)
                spilled_py.append(gq.n)
            prev_evac_v = vv.n

        self.py_done_g = gq.n
        if "py" in self.DBG:
            for t in range(NTILE):
                @block.sync
                def _(sync, t=t, gqn=gq.n):
                    sync.wait_ge(gq.sem, gqn)
                    ld.inc(sync.dma_start(out=xb2[0], in_=self.py_sp[t * 128:(t + 1) * 128, :]), 16)

                @block.gpsimd
                def _(gpsimd, t=t, ldn=ld.n):
                    gpsimd.wait_ge(ld.sem, ldn)
                    gq.inc(gpsimd.dma_start(out=self.DBG["py"][t * 128:(t + 1) * 128, :], in_=xb2[0]), 16)
        al.pop()

    def phase_F(self):
        nc, al, block = self.nc, self.al, self.block
        ld, gq, tt, vv, aa, cc = self.ld, self.gq, self.tt, self.vv, self.aa, self.cc
        psum, stat, tiny, ident = self.psum, self.stat, self.tiny, self.ident

        al.push()
        xqs_f = al.take(2048)
        xqs = xqs_f.bitcast(BF16)
        pyb = [al.take(D) for _ in range(2)]
        hid = [al.take(FF) for _ in range(2)]
        g8 = [al.take(2048, FP8) for _ in range(2)]
        u8 = [al.take(2048, FP8) for _ in range(2)]
        d8 = [al.take(DOWN_KC * 128, FP8) for _ in range(2)]
        rg_sb = al.take(512)
        uu_sb = al.take(512)
        sq_sb = al.take(512)
        q3f = al.take(512)
        q3i = al.take(512, I32)
        q3b = al.take(256, BF16)
        xq3T = al.take(DOWN_KC * 256 // 2, BF16)
        ybounce = [al.take(512) for _ in range(2)]
        mxall = al.take(32)
        invg = stat[:, 72:80]
        invd_t = stat[:, 80:88]
        s3c = stat[:, 88:89]
        i3c = stat[:, 89:90]

        xq3v = xq3T.rearrange("p (k t) -> p k t", k=DOWN_KC)
        xqv = xqs.rearrange("p (k t) -> p k t", k=KC)

        def gate_rows(cg):
            r, idx = cg // SH_GCH, cg % SH_GCH
            return 66 * r + idx

        def down_rows(cd):
            r, idx = cd // SH_DCH, cd % SH_DCH
            return 66 * r + 44 + idx

        @block.vector
        def _(vector, pyg=self.py_done_g):
            vector.wait_ge(gq.sem, pyg)
            vector.tensor_scalar(out=invg, in0=stat[:, 64:72], scalar1=tiny[:, 2:3],
                                 scalar2=None, op0=ALU.mult)
            vv.inc(vector.drain() if False else vector.tensor_copy(out=stat[:, 80:88], in_=invg))
            vector.drain()
        scales_f = vv.n

        last_down_tt = 0
        ystored = []
        yevac = []
        for s in range(FFN_SLABS):
            @block.sync
            def _(sync, s=s, sf=scales_f, ldt=last_down_tt):
                sync.wait_ge(vv.sem, sf)
                if ldt:
                    sync.wait_ge(tt.sem, ldt)
                sync.dma_start(out=xqs_f.rearrange("p (k t) -> p k t", k=KC),
                               in_=self.xq_sp.rearrange("p (k t) -> p k t", k=KC)[:, :, s * 256:(s + 1) * 256].bitcast(F32)).then_inc(ld.sem, 16)
                ld.n += 16
                for j in range(2):
                    ld.inc(sync.dma_start(out=pyb[j], in_=self.py_sp[(2 * s + j) * 128:(2 * s + j + 1) * 128, :]), 16)
            slab_ld = ld.n

            hid_v = []
            pair_tt = []
            for jp in range(11):
                gate_cg0 = jp * 16
                up_cg0 = (jp + 11) * 16

                @block.sync
                def _(sync, s=s, jp=jp, slab_ld=slab_ld,
                      hv=(hid_v[-2] if len(hid_v) >= 2 else 0), tcc=self.tern_cc):
                    if s == 0 and jp == 0:
                        sync.wait_ge(cc.sem, tcc)
                    if jp >= 2 and hv:
                        sync.wait_ge(vv.sem, hv)
                    for base, buf in ((gate_cg0, g8[jp % 2]), (up_cg0, u8[jp % 2])):
                        rows = [gate_rows(base + k) for k in range(16)]
                        runs = []
                        st0 = 0
                        for k in range(1, 17):
                            if k == 16 or rows[k] != rows[k - 1] + 1:
                                runs.append((st0, k))
                                st0 = k
                        off = 0
                        for (a, b) in runs:
                            n = b - a
                            sync.dma_start(
                                out=buf[:, off * 512:(off + n) * 512].bitcast(F32).rearrange(
                                    "p (c e) -> p c e", c=n),
                                in_=self.tern_out[rows[a]:rows[a] + n].bitcast(F32).rearrange(
                                    "c p e -> p c e")).then_inc(ld.sem, 16)
                            ld.n += 16
                            off += n
                pair_ld = ld.n

                for t in range(2):
                    @block.tensor
                    def _(tensor, s=s, jp=jp, t=t, pair_ld=pair_ld, slab_ld=slab_ld,
                          hvv=(hid_v[-1] if (jp > 0 or t > 0) and hid_v else 0), jp0=(jp == 0 and t == 0)):
                        tensor.wait_ge(ld.sem, pair_ld)
                        if jp0:
                            tensor.wait_ge(ld.sem, slab_ld)
                        if hvv:
                            tensor.wait_ge(vv.sem, hvv)
                        for k in range(KC):
                            lhs = xqv[:, k, t * 128:(t + 1) * 128]
                            tensor.matmul(psum[2 * t], lhs,
                                          g8[jp % 2][:, k * 512:(k + 1) * 512],
                                          start=(k == 0), stop=(k == KC - 1))
                        for k in range(KC):
                            lhs = xqv[:, k, t * 128:(t + 1) * 128]
                            mm = tensor.matmul(psum[2 * t + 1], lhs,
                                               u8[jp % 2][:, k * 512:(k + 1) * 512],
                                               start=(k == 0), stop=(k == KC - 1))
                            if k == KC - 1:
                                tt.inc(mm)
                    gu_tt = tt.n
                    pair_tt.append(gu_tt)

                    @block.scalar
                    def _(scalar, s=s, t=t, gu_tt=gu_tt):
                        scalar.wait_ge(tt.sem, gu_tt)
                        tile = 2 * s + t
                        aa.inc(scalar.activation(out=rg_sb, in_=psum[2 * t], func=AF.Relu,
                                                 scale=invg[:, tile:tile + 1]))
                    gu_aa = aa.n

                    @block.vector
                    def _(vector, s=s, jp=jp, t=t, gu_aa=gu_aa, gu_tt=gu_tt):
                        vector.wait_ge(aa.sem, gu_aa)
                        vector.wait_ge(tt.sem, gu_tt)
                        tile = 2 * s + t
                        vector.tensor_scalar(out=uu_sb, in0=psum[2 * t + 1],
                                             scalar1=invg[:, tile:tile + 1], scalar2=None, op0=ALU.mult)
                        vector.tensor_tensor(out=sq_sb, in0=rg_sb, in1=rg_sb, op=ALU.mult)
                        vector.drain()
                        vv.inc(vector.tensor_tensor(out=hid[t][:, jp * 512:(jp + 1) * 512],
                                                    in0=sq_sb, in1=uu_sb, op=ALU.mult))
                        vector.drain()
                    hid_v.append(vv.n)

            xq3_v = []
            for t in range(2):
                @block.vector
                def _(vector, s=s, t=t, hv=hid_v[-1]):
                    vector.wait_ge(vv.sem, hv)
                    t_d, t_e = stat[:, 88:89], stat[:, 89:90]
                    for p in range(11):
                        vector.tensor_scalar(out=sq_sb, in0=hid[t][:, p * 512:(p + 1) * 512],
                                             scalar1=1.0, scalar2=0.0,
                                             op0=ALU.mult, op1=ALU.max, accum_out=mxall[:, p:p + 1])
                        vector.tensor_scalar(out=sq_sb, in0=hid[t][:, p * 512:(p + 1) * 512],
                                             scalar1=-1.0, scalar2=0.0,
                                             op0=ALU.mult, op1=ALU.max, accum_out=mxall[:, 11 + p:12 + p])
                    vector.drain()
                    vector.tensor_reduce(out=t_d, in_=mxall[:, 0:22], axis=AX.X, op=ALU.max)
                    vector.drain()
                    vector.tensor_scalar(out=t_d, in0=t_d, scalar1=1e-5, scalar2=None, op0=ALU.max)
                    vector.drain()
                    vector.reciprocal(out=t_e, in_=t_d)
                    vector.drain()
                    vector.tensor_scalar(out=stat[:, 92 + t:93 + t], in0=t_e, scalar1=127.0,
                                         scalar2=None, op0=ALU.mult)
                    vector.tensor_scalar(out=t_d, in0=t_d, scalar1=1.0 / 127.0, scalar2=None, op0=ALU.mult)
                    vector.drain()
                    vv.inc(vector.tensor_tensor(out=stat[:, 94 + t:95 + t], in0=t_d, in1=tiny[:, 3:4], op=ALU.mult))
                    vector.drain()
                q_v = vv.n

                for piece in range(11):
                    @block.vector
                    def _(vector, t=t, piece=piece, q_v=q_v, ptt=(tt.n if piece >= 1 else 0)):
                        if piece == 0:
                            vector.wait_ge(vv.sem, q_v)
                        if ptt:
                            vector.wait_ge(tt.sem, ptt)
                        vector.tensor_scalar(out=q3f, in0=hid[t][:, piece * 512:(piece + 1) * 512],
                                             scalar1=stat[:, 92 + t:93 + t], scalar2=None, op0=ALU.mult)
                        vector.drain()
                        vector.tensor_copy(out=q3i, in_=q3f)
                        vector.drain()
                        vv.inc(vector.tensor_copy(out=q3b[:, 0:512], in_=q3i))
                    p_v = vv.n

                    @block.tensor
                    def _(tensor, t=t, piece=piece, p_v=p_v, pev=(vv.n if piece >= 1 else 0)):
                        tensor.wait_ge(vv.sem, p_v)
                        for j in range(4):
                            k2 = piece * 4 + j
                            mm = tensor.transpose(psum[4 + (piece % 2)][:, j * 64:(j + 1) * 64].bitcast(BF16),
                                                  q3b[:, j * 128:(j + 1) * 128], self.ident_bf[:, 0:128])
                            if j == 3:
                                tt.inc(mm)
                    tp_tt = tt.n

                    @block.vector
                    def _(vector, t=t, piece=piece, tp_tt=tp_tt):
                        vector.wait_ge(tt.sem, tp_tt)
                        vv.inc(vector.tensor_copy(
                            out=xq3v[:, piece * 4:piece * 4 + 4, t * 128:(t + 1) * 128],
                            in_=psum[4 + (piece % 2)].bitcast(BF16).rearrange("p (k e) -> p k e", k=8)[:, 0:4, :]))
                xq3_v.append(vv.n)

            down_tts = []
            for op2 in range(4):
                @block.sync
                def _(sync, s=s, op2=op2, xv=xq3_v[-1],
                      dprev=(down_tts[op2 - 2] if op2 >= 2 else (last_down_tt if op2 < 2 else 0))):
                    sync.wait_ge(vv.sem, xv)
                    if dprev:
                        sync.wait_ge(tt.sem, dprev)
                    rows0 = down_rows(op2 * DOWN_KC)
                    for half in range(2):
                        r0 = down_rows(op2 * DOWN_KC + half * 22)
                        sync.dma_start(
                            out=d8[op2 % 2][:, half * 22 * 512:(half + 1) * 22 * 512].bitcast(F32).rearrange(
                                "p (c e) -> p c e", c=22),
                            in_=self.tern_out[r0:r0 + 22].bitcast(F32).rearrange("c p e -> p c e")).then_inc(ld.sem, 16)
                        ld.n += 16
                down_ld = ld.n

                for t in range(2):
                    @block.tensor
                    def _(tensor, s=s, op2=op2, t=t, down_ld=down_ld,
                          yv=(yevac[-2] if len(yevac) >= 2 else 0)):
                        tensor.wait_ge(ld.sem, down_ld)
                        if yv:
                            tensor.wait_ge(vv.sem, yv)
                        for k2 in range(DOWN_KC):
                            mm = tensor.matmul(psum[6 + t], xq3v[:, k2, t * 128:(t + 1) * 128],
                                               d8[op2 % 2][:, k2 * 512:(k2 + 1) * 512],
                                               start=(k2 == 0), stop=(k2 == DOWN_KC - 1))
                            if k2 == DOWN_KC - 1:
                                tt.inc(mm)
                    dn_tt = tt.n
                    last_down_tt = dn_tt
                    if t == 1:
                        down_tts.append(dn_tt)

                    @block.vector
                    def _(vector, s=s, op2=op2, t=t, dn_tt=dn_tt,
                          ysg=(ystored[-2] if len(ystored) >= 2 else 0)):
                        vector.wait_ge(tt.sem, dn_tt)
                        if ysg:
                            vector.wait_ge(self.st.sem, ysg)
                        vv.inc(vector.scalar_tensor_tensor(
                            out=ybounce[t % 2], in0=psum[6 + t], scalar=stat[:, 94 + t:95 + t],
                            in1=pyb[t][:, op2 * 512:(op2 + 1) * 512], op0=ALU.mult, op1=ALU.add))
                    y_v = vv.n
                    yevac.append(y_v)

                    @block.gpsimd
                    def _(gpsimd, s=s, op2=op2, t=t, y_v=y_v):
                        gpsimd.wait_ge(vv.sem, y_v)
                        self.st.inc(gpsimd.dma_start(
                            out=self.yout[(2 * s + t) * 128:(2 * s + t + 1) * 128,
                                          op2 * 512:(op2 + 1) * 512], in_=ybounce[t % 2]), 16)
                    ystored.append(self.st.n)
        al.pop()

    def stub_out(self):
        al, block = self.al, self.block
        ld, vv = self.ld, self.vv
        al.push()
        buf = [al.take(D) for _ in range(2)]
        stored = []
        for i in range(NTILE):
            @block.sync
            def _(sync, i=i, ccn=self.cc.n):
                if i < 2:
                    sync.wait_ge(self.gq.sem, self.gq.n)
                    sync.wait_ge(self.cc.sem, ccn)
                else:
                    sync.wait_ge(self.st.sem, stored[i - 2])
                ld.inc(sync.dma_start(out=buf[i % 2], in_=self.xin[i * 128:(i + 1) * 128, :]), 16)

            @block.gpsimd
            def _(gpsimd, i=i):
                gpsimd.wait_ge(ld.sem, self.ld.n)
                self.st.inc(gpsimd.dma_start(out=self.yout[i * 128:(i + 1) * 128, :], in_=buf[i % 2]), 16)
            stored.append(self.st.n)
        al.pop()

    def final_wait(self):
        block = self.block

        @block.gpsimd
        def _(gpsimd):
            gpsimd.wait_ge(self.st.sem, self.st.n)
            gpsimd.wait_ge(self.gq.sem, self.gq.n)
            if self.upto >= 1:
                gpsimd.wait_ge(self.cc.sem, self.cc.n)


def build_nc(upto="F", debug=()):
    b = Builder(upto=upto, debug=debug)
    return b.build(), b.DBG


def kernel(**inputs):
    in_maps = _prep(inputs)
    nc, _ = build_nc(upto="F")
    res = run_bass_kernel_spmd(nc, in_maps, core_ids=list(range(NCORES)))
    y = np.concatenate([r["yout"] for r in res.results], axis=0)
    return np.ascontiguousarray(y).reshape(1, NCORES * TOK, D)


# version 07
# speedup vs baseline: 1.1815x; 1.1815x over previous
import numpy as np
import concourse.bass as bass
import concourse.mybir as mybir
from concourse.bass_utils import run_bass_kernel_spmd
from contextlib import ExitStack

F32 = mybir.dt.float32
BF16 = mybir.dt.bfloat16
I32 = mybir.dt.int32
FP8 = mybir.dt.float8e4
AF = mybir.ActivationFunctionType
ALU = mybir.AluOpType
AX = mybir.AxisListType

NCORES = 8
TOK = 1024
D = 2048
H = 16
HD = 128
CH = 128
NCHUNK = TOK // CH
KC = D // 128
NTILE = TOK // 128
FF = 5632
FF2 = 2 * FF
EPS = 1e-5
INV_D = 1.0 / D

GATE_ELEMS = FF2 * D
DOWN_ELEMS = D * FF
GATE_CHUNKS = GATE_ELEMS // 65536
DOWN_CHUNKS = DOWN_ELEMS // 65536
SH_GCH = GATE_CHUNKS // NCORES
SH_DCH = DOWN_CHUNKS // NCORES
SH_CH = SH_GCH + SH_DCH
GATE_PANELS = FF2 // 512
DOWN_PANELS = D // 512
DOWN_KC = FF // 128
FFN_SLABS = 4
SLAB = TOK // FFN_SLABS
SLAB_TILES = SLAB // 128


def _host_consts(core):
    r = np.exp(-(2.0 ** (-0.5 * (np.arange(H, dtype=np.float64) + 1.0))))
    i = np.arange(CH, dtype=np.float64)
    qdec = r[:, None] ** (i[None, :] + 1.0)
    kdec = r[:, None] ** (CH - 1.0 - i[None, :])
    diff = i[:, None] - i[None, :]
    dmask = np.where(diff[None] >= 0, r[:, None, None] ** np.maximum(diff[None], 0.0), 0.0)
    dmaskT = np.transpose(dmask, (0, 2, 1))
    rC = r ** float(CH)

    dmask4 = np.zeros((4, 128, 512), dtype=np.float32)
    qdec4 = np.zeros((4, 128, 512), dtype=np.float32)
    for g in range(4):
        for hq in range(4):
            h = 4 * g + hq
            dmask4[g, :, hq * 128:(hq + 1) * 128] = dmaskT[h].astype(np.float32)
            qdec4[g, :, hq * 128:(hq + 1) * 128] = np.broadcast_to(
                qdec[h][None, :], (128, 128)).astype(np.float32)
    kdecb = np.zeros((128, D), dtype=np.float32)
    for h in range(H):
        kdecb[:, h * 128:(h + 1) * 128] = kdec[h][:, None].astype(np.float32)

    ex = np.linspace(0.0, 1.0, HD // 4, dtype=np.float32)
    freq = np.power(np.float32(1.0 / 1024.0), ex).astype(np.float32)
    t_abs = (core * TOK + np.arange(TOK)).astype(np.float32)
    theta = (t_abs[:, None] * freq[None, :]).astype(np.float32)
    cosv = np.cos(theta.astype(np.float64)).astype(np.float32)
    sinv = np.sin(theta.astype(np.float64)).astype(np.float32)
    cosrep = np.ascontiguousarray(np.tile(cosv, (1, H)).reshape(NCHUNK, 128, H * 32))
    sinrep = np.ascontiguousarray(np.tile(sinv, (1, H)).reshape(NCHUNK, 128, H * 32))

    zw = np.zeros((128, 128), dtype=np.float32)
    for cp in range(core):
        for h in range(H):
            zw[:, cp * 16 + h] = np.float32(rC[h] ** (8.0 * (core - 1 - cp)))
    rc_pow = np.zeros((H, NCHUNK), dtype=np.float64)
    for h in range(H):
        for l in range(NCHUNK):
            rc_pow[h, l] = rC[h] ** l
    def fl(a):
        a = np.asarray(a)
        return np.where(np.abs(a) < 1.2e-38, 0.0, a)
    return dict(dmask4=fl(dmask4).astype(np.float32), qdec4=fl(qdec4).astype(np.float32),
                kdecb=fl(kdecb).astype(np.float32), cosrep=cosrep, sinrep=sinrep,
                zw=fl(zw).astype(np.float32), rc_pow=fl(rc_pow), rC=fl(rC))


def _prep(inputs):
    x_full = np.ascontiguousarray(np.asarray(inputs["L0_attn_x"], dtype=np.float32)[0])
    Wq = np.asarray(inputs["L0_attn_Wq"], dtype=np.float32)
    Wk = np.asarray(inputs["L0_attn_Wk"], dtype=np.float32)
    Wv = np.asarray(inputs["L0_attn_Wv"], dtype=np.float32)
    Wo = np.asarray(inputs["L0_attn_Wo"], dtype=np.float32)
    rms_w = np.asarray(inputs["L0_attn_rms_w"], dtype=np.float32)
    wgu = np.asarray(inputs["L0_ffn_w_gate_up"], dtype=np.float32)
    wd = np.asarray(inputs["L0_ffn_w_down"], dtype=np.float32)
    wnorm = np.asarray(inputs["L0_ffn_norm_weight"], dtype=np.float32)

    def tile_w(W):
        return np.ascontiguousarray(W.reshape(KC, 128, 4, 512).transpose(2, 0, 1, 3))

    wq_t, wk_t, wv_t, wo_t = tile_w(Wq), tile_w(Wk), tile_w(Wv), tile_w(Wo)
    wguT = np.ascontiguousarray(wgu.T)
    wgu_t = np.ascontiguousarray(wguT.reshape(KC, 128, GATE_PANELS, 512).transpose(2, 0, 1, 3))
    wdT = np.ascontiguousarray(wd.T)
    wd_t = np.ascontiguousarray(wdT.reshape(DOWN_KC, 128, DOWN_PANELS, 512).transpose(2, 0, 1, 3))
    gate_ch = wgu_t.reshape(GATE_CHUNKS, 128, 512)
    down_ch = wd_t.reshape(DOWN_CHUNKS, 128, 512)

    rms_pp = np.ascontiguousarray(rms_w.reshape(KC, 128).T)
    wnorm_rep = np.ascontiguousarray(np.broadcast_to(wnorm[None, :], (128, D)))
    ident = np.eye(128, dtype=np.float32)

    in_maps = []
    for c in range(NCORES):
        cc = _host_consts(c)
        shard = np.ascontiguousarray(np.concatenate([
            gate_ch[c * SH_GCH:(c + 1) * SH_GCH],
            down_ch[c * SH_DCH:(c + 1) * SH_DCH]], axis=0))
        in_maps.append({
            "xin": np.array(x_full[c * TOK:(c + 1) * TOK], copy=True),
            "wq_t": wq_t, "wk_t": wk_t, "wv_t": wv_t, "wo_t": wo_t,
            "wshard": shard, "rms_pp": rms_pp, "wnorm_rep": wnorm_rep, "ident": ident,
            "dmask4": cc["dmask4"], "qdec4": cc["qdec4"], "kdecb": cc["kdecb"],
            "cosrep": cc["cosrep"], "sinrep": cc["sinrep"], "zw": cc["zw"],
        })
    return in_maps


class Trk:
    __slots__ = ("sem", "n")

    def __init__(self, sem):
        self.sem = sem
        self.n = 0

    def inc(self, instr, k=1):
        instr.then_inc(self.sem, k)
        self.n += k
        return self.n


class Alloc:
    def __init__(self, arena, total):
        self.arena, self.total, self.off, self.marks = arena, total, 0, []

    def take(self, n_fp32, dtype=F32):
        n_fp32 = (n_fp32 + 127) & ~127
        assert self.off + n_fp32 <= self.total, f"arena overflow {self.off}+{n_fp32}>{self.total}"
        ap = self.arena[:, self.off:self.off + n_fp32]
        self.off += n_fp32
        return ap if dtype == F32 else ap.bitcast(dtype)

    def push(self):
        self.marks.append(self.off)

    def pop(self):
        self.off = self.marks.pop()


class Builder:
    def __init__(self, upto="F", debug=()):
        self.upto = {"L": 0, "W": 1, "P": 2, "A": 3, "C": 4, "F": 5}[upto]
        self.dbg = set(debug)
        self.DBG = {}
        self.rc = _host_consts(0)

    def declare(self, nc):
        d = lambda *a, **k: nc.dram_tensor(*a, **k).ap()
        self.xin = d("xin", [TOK, D], F32, kind="ExternalInput")
        self.w_t = {
            "q": d("wq_t", [4, KC, 128, 512], F32, kind="ExternalInput"),
            "k": d("wk_t", [4, KC, 128, 512], F32, kind="ExternalInput"),
            "v": d("wv_t", [4, KC, 128, 512], F32, kind="ExternalInput"),
            "o": d("wo_t", [4, KC, 128, 512], F32, kind="ExternalInput"),
        }
        self.wshard = d("wshard", [SH_CH, 128, 512], F32, kind="ExternalInput")
        self.rms_pp_d = d("rms_pp", [128, KC], F32, kind="ExternalInput")
        self.wnorm_d = d("wnorm_rep", [128, D], F32, kind="ExternalInput")
        self.ident_d = d("ident", [128, 128], F32, kind="ExternalInput")
        self.dmask4_d = d("dmask4", [4, 128, 512], F32, kind="ExternalInput")
        self.qdec4_d = d("qdec4", [4, 128, 512], F32, kind="ExternalInput")
        self.kdecb_d = d("kdecb", [128, D], F32, kind="ExternalInput")
        self.cosrep_d = d("cosrep", [NCHUNK, 128, 512], F32, kind="ExternalInput")
        self.sinrep_d = d("sinrep", [NCHUNK, 128, 512], F32, kind="ExternalInput")
        self.zw_d = d("zw", [128, 128], F32, kind="ExternalInput")
        self.yout = d("yout", [TOK, D], F32, kind="ExternalOutput")

        self.xq_sp = nc.dram_tensor("xq_sp", [128, KC * TOK], BF16).ap()
        self.qkv_sp = {nm: nc.dram_tensor(nm + "_sp", [TOK, D], F32).ap() for nm in ("q", "k", "v")}
        self.attn_sp = nc.dram_tensor("attn_sp", [NCHUNK, 128, D], F32).ap()
        self.qd_sp = nc.dram_tensor("qd_sp", [NCHUNK, 128, D], F32).ap()
        self.ar_in = nc.dram_tensor("ar_in", [1, 16], F32).ap()
        self.ar_out = nc.dram_tensor("ar_out", [1, 16], F32).ap()
        self.tern_in = nc.dram_tensor("tern_in", [SH_CH, 128, 512], FP8).ap()
        self.tern_out = nc.dram_tensor("tern_out", [NCORES * SH_CH, 128, 512], FP8).ap()
        self.z_in = nc.dram_tensor("z_in", [128, D], F32).ap()
        self.z_out = nc.dram_tensor("z_out", [NCORES * 128, D], F32).ap()

        for name, shape in [("xh", [128, KC * TOK]), ("xl", [128, KC * TOK]),
                            ("xq", [128, KC * TOK]), ("stats", [128, 96]),
                            ("swv", [1, 16]), ("qkv", [3, TOK, D]),
                            ("attn", [NCHUNK, 128, D]), ("z", [NCORES * 128, D]),
                            ("attnc", [NCHUNK, 128, D]), ("py", [TOK, D]),
                            ("rr", [128, 16]), ("rrow", [1, 2048]),
                            ("tern", [SH_CH, 128, 512]), ("hid", [SLAB, FF])]:
            if name in self.dbg:
                self.DBG[name] = nc.dram_tensor("dbg_" + name, shape, F32, kind="ExternalOutput").ap()

    def build(self):
        nc = bass.Bass("TRN2", target_bir_lowering=False)
        self.nc = nc
        self.declare(nc)
        with ExitStack() as stack:
            E = stack.enter_context
            ARENA_N = 50176
            self.arena = E(nc.sbuf_tensor("arena", [128, ARENA_N], F32))[:]
            self.al = Alloc(self.arena, ARENA_N)
            self.ident = E(nc.sbuf_tensor("identt", [128, 128], F32))[:]
            self.ident_bf = E(nc.sbuf_tensor("identb", [128, 64], F32))[:].bitcast(BF16)
            self.stat = E(nc.sbuf_tensor("stat", [128, 96], F32))[:]
            self.tiny = E(nc.sbuf_tensor("tiny", [128, 16], F32))[:]
            self.rrv = E(nc.sbuf_tensor("rrv", [128, NCHUNK], F32))[:]
            self.onesr = E(nc.sbuf_tensor("onesr", [1, 128], F32))[:]
            self.arbuf = E(nc.sbuf_tensor("arbuf", [1, 16], F32))[:]
            self.kvst = E(nc.sbuf_tensor("kvst", [128, D], F32))[:]
            self.psum = [E(nc.psum_tensor(f"pb{i}", [128, 512], F32))[:] for i in range(8)]
            sems = {}
            for s in ("s_ld", "s_g", "s_q2", "s_t", "s_v", "s_a", "s_cc", "s_st"):
                sems[s] = E(nc.semaphore(s))
            self.block = E(nc.Block())
            self.ld = Trk(sems["s_ld"])
            self.gq = Trk(sems["s_g"])
            self.q2 = Trk(sems["s_q2"])
            self.tt = Trk(sems["s_t"])
            self.vv = Trk(sems["s_v"])
            self.aa = Trk(sems["s_a"])
            self.cc = Trk(sems["s_cc"])
            self.st = Trk(sems["s_st"])

            self.al.push()
            self.xh_sb = self.al.take(KC * TOK // 2, BF16)
            self.xl_sb = self.al.take(KC * TOK // 2, BF16)
            self.xh3 = self.xh_sb.rearrange("p (k t) -> p k t", k=KC)
            self.xl3 = self.xl_sb.rearrange("p (k t) -> p k t", k=KC)
            self.phase_L()
            if self.upto >= 2:
                self.phase_P()
            self.al.pop()
            self.al.push()
            self.wb = {
                "shbuf": [self.al.take(2048) for _ in range(2)],
                "posb": self.al.take(512, FP8),
                "negb": self.al.take(512, FP8),
                "terb": [self.al.take(512, FP8) for _ in range(2)],
                "pr2": self.al.take(128),
            }
            if self.upto >= 1:
                self.phase_W()
            if self.upto >= 3:
                self.phase_A()
            self.al.pop()
            if self.upto >= 4:
                self.phase_C()
            if self.upto >= 5:
                self.phase_F()
            else:
                self.stub_out()
            self.final_wait()
        return nc

    def phase_L(self):
        nc, al, block = self.nc, self.al, self.block
        ld, tt, vv, aa, gq = self.ld, self.tt, self.vv, self.aa, self.gq
        psum, stat, ident = self.psum, self.stat, self.ident

        al.push()
        xbuf = [al.take(D) for _ in range(2)]
        scr1 = al.take(D)
        scr2 = al.take(D)
        xn_i = al.take(D)
        t1b = al.take(D)
        xqf = al.take(D)
        xqi = al.take(D, I32)
        xqb = al.take(D // 2, BF16)
        self.wnorm_s = wnorm_s = al.take(D)
        xqT_sb = al.take(KC * TOK // 2, BF16)
        xh3, xl3 = self.xh3, self.xl3
        xq3 = xqT_sb.rearrange("p (k t) -> p k t", k=KC)

        @block.sync
        def _(sync):
            ld.inc(sync.dma_start(out=self.ident[:], in_=self.ident_d), 16)
            ld.inc(sync.dma_start(out=wnorm_s, in_=self.wnorm_d), 16)
            ld.inc(sync.dma_start(out=xbuf[0], in_=self.xin[0:128, :]), 16)
            ld.inc(sync.dma_start(out=xbuf[1], in_=self.xin[128:256, :]), 16)
        x1_ld = [ld.n - 32 + 16 * (i + 1) for i in range(2)]

        consumed = [0] * NTILE
        for i in range(NTILE):
            @block.vector
            def _(vector, i=i):
                vector.wait_ge(ld.sem, x1_ld[i])
                if i == 0:
                    vector.memset(self.onesr[:], 1.0)
                    vector.tensor_copy(out=self.ident_bf[:, 0:128], in_=ident[:])
                vv.inc(vector.tensor_scalar(out=scr1, in0=xbuf[i % 2], scalar1=1.0, scalar2=0.0,
                                            op0=ALU.mult, op1=ALU.add, accum_out=stat[:, 0 + i:1 + i]))

            @block.scalar
            def _(scalar, i=i):
                scalar.wait_ge(ld.sem, x1_ld[i])
                scalar.wait_ge(vv.sem, vv.n)
                aa.inc(scalar.activation(out=scr2, in_=xbuf[i % 2], func=AF.Square,
                                         accum_out=stat[:, 8 + i:9 + i]))
            consumed[i] = aa.n

            if i + 2 < NTILE:
                @block.sync
                def _(sync, i=i):
                    sync.wait_ge(aa.sem, consumed[i])
                    ld.inc(sync.dma_start(out=xbuf[i % 2], in_=self.xin[(i + 2) * 128:(i + 3) * 128, :]), 16)
                x1_ld.append(ld.n)

        @block.vector
        def _(vector):
            vector.wait_ge(aa.sem, aa.n)
            vector.drain()
            m = stat[:, 16:24]
            t_a, t_b = stat[:, 72:80], stat[:, 80:88]
            vector.tensor_scalar(out=m, in0=stat[:, 0:8], scalar1=INV_D, scalar2=None, op0=ALU.mult)
            vector.tensor_scalar(out=t_a, in0=stat[:, 8:16], scalar1=INV_D, scalar2=None, op0=ALU.mult)
            vector.drain()
            vector.tensor_tensor(out=t_b, in0=m, in1=m, op=ALU.mult)
            vector.drain()
            vector.tensor_tensor(out=t_b, in0=t_a, in1=t_b, op=ALU.subtract)
            vector.drain()
            vv.inc(vector.tensor_scalar(out=stat[:, 32:40], in0=t_b, scalar1=EPS, scalar2=None, op0=ALU.add))

        @block.scalar
        def _(scalar):
            scalar.wait_ge(vv.sem, vv.n)
            aa.inc(scalar.activation(out=stat[:, 88:96], in_=stat[:, 32:40], func=AF.Sqrt))

        @block.vector
        def _(vector):
            vector.wait_ge(aa.sem, aa.n)
            inv, var = stat[:, 24:32], stat[:, 80:88]
            vector.reciprocal(out=inv, in_=stat[:, 88:96])
            vector.drain()
            vector.tensor_tensor(out=stat[:, 40:48], in0=var, in1=inv, op=ALU.mult)
            vector.drain()
            vector.tensor_tensor(out=stat[:, 40:48], in0=stat[:, 40:48], in1=inv, op=ALU.mult)
            vector.drain()
            vv.inc(vector.tensor_scalar(out=stat[:, 40:48], in0=stat[:, 40:48],
                                        scalar1=EPS, scalar2=None, op0=ALU.add))

        @block.scalar
        def _(scalar):
            scalar.wait_ge(vv.sem, vv.n)
            aa.inc(scalar.activation(out=stat[:, 88:96], in_=stat[:, 40:48], func=AF.Sqrt))

        @block.vector
        def _(vector):
            vector.wait_ge(aa.sem, aa.n)
            vv.inc(vector.reciprocal(out=stat[:, 48:56], in_=stat[:, 88:96]))

        @block.sync
        def _(sync):
            sync.wait_ge(vv.sem, vv.n)
            ld.inc(sync.dma_start(out=xbuf[0], in_=self.xin[0:128, :]), 16)
            ld.inc(sync.dma_start(out=xbuf[1], in_=self.xin[128:256, :]), 16)
        x2_ld = [ld.n - 32 + 16 * (i + 1) for i in range(2)]

        xdone = [0] * NTILE
        for i in range(NTILE):
            ts = i * 128

            @block.vector
            def _(vector, i=i, ts=ts):
                vector.wait_ge(ld.sem, x2_ld[i])
                if i > 0:
                    vector.wait_ge(tt.sem, tposed[i - 1])
                s_m, s_inv = stat[:, 16 + i:17 + i], stat[:, 24 + i:25 + i]
                s_rf = stat[:, 48 + i:49 + i]
                s_A, s_is2 = stat[:, 56 + i:57 + i], stat[:, 64 + i:65 + i]
                t_d, t_e = stat[:, 88:89], stat[:, 89:90]
                vector.tensor_scalar(out=xn_i, in0=xbuf[i % 2], scalar1=s_m, scalar2=s_inv,
                                     op0=ALU.subtract, op1=ALU.mult)
                vector.drain()
                vector.tensor_tensor(out=t1b, in0=xn_i, in1=wnorm_s, op=ALU.mult)
                vector.drain()
                vector.tensor_scalar(out=scr1, in0=t1b, scalar1=1.0, scalar2=0.0,
                                     op0=ALU.mult, op1=ALU.max, accum_out=t_d)
                vector.tensor_scalar(out=scr1, in0=t1b, scalar1=-1.0, scalar2=0.0,
                                     op0=ALU.mult, op1=ALU.max, accum_out=t_e)
                vector.drain()
                vector.tensor_tensor(out=t_d, in0=t_d, in1=t_e, op=ALU.max)
                vector.drain()
                vector.tensor_tensor(out=t_d, in0=t_d, in1=s_rf, op=ALU.mult)
                vector.drain()
                vector.tensor_scalar(out=t_d, in0=t_d, scalar1=1e-5, scalar2=None, op0=ALU.max)
                vector.drain()
                vector.reciprocal(out=t_e, in_=t_d)
                vector.drain()
                vector.tensor_scalar(out=t_e, in0=t_e, scalar1=127.0, scalar2=None, op0=ALU.mult)
                vector.drain()
                vector.tensor_tensor(out=s_A, in0=s_rf, in1=t_e, op=ALU.mult)
                vector.tensor_scalar(out=s_is2, in0=t_d, scalar1=1.0 / 127.0, scalar2=None, op0=ALU.mult)
                vector.drain()
                vector.tensor_scalar(out=xqf, in0=t1b, scalar1=s_A, scalar2=None, op0=ALU.mult)
                vector.drain()
                vector.tensor_copy(out=xqi, in_=xqf)
                vector.drain()
                vv.inc(vector.tensor_copy(out=xqb, in_=xqi))
            xdone[i] = vv.n

            if i + 2 < NTILE:
                @block.sync
                def _(sync, i=i):
                    sync.wait_ge(vv.sem, xdone[i])
                    ld.inc(sync.dma_start(out=xbuf[i % 2], in_=self.xin[(i + 2) * 128:(i + 3) * 128, :]), 16)
                x2_ld.append(ld.n)

            if i == 0:
                tposed = [0] * NTILE

            @block.tensor
            def _(tensor, i=i):
                tensor.wait_ge(vv.sem, xdone[i])
                if i > 0:
                    tensor.wait_ge(vv.sem, evacd_v[i - 1])
                    tensor.wait_ge(aa.sem, evacd_a[i - 1])
                for b in range(4):
                    for j in range(4):
                        k = 4 * b + j
                        mm = tensor.transpose(psum[b][:, j * 128:(j + 1) * 128],
                                              xn_i[:, k * 128:(k + 1) * 128], ident)
                        if j == 3:
                            tt.inc(mm)
                for b in range(2):
                    for j in range(8):
                        k = 8 * b + j
                        mm = tensor.transpose(psum[4 + b][:, j * 64:(j + 1) * 64].bitcast(BF16),
                                              xqb[:, k * 128:(k + 1) * 128], self.ident_bf[:, 0:128])
                        if j == 7:
                            tt.inc(mm)
            tposed[i] = tt.n

            if i == 0:
                evacd_v, evacd_a = [0] * NTILE, [0] * NTILE

            @block.scalar
            def _(scalar, i=i, ts=ts):
                for b in range(4):
                    scalar.wait_ge(tt.sem, tposed[i] - 6 + b + 1)
                    aa.inc(scalar.activation(out=xh3[:, 4 * b:4 * b + 4, ts:ts + 128],
                                             in_=psum[b].rearrange("p (k t) -> p k t", k=4),
                                             func=AF.Copy))
            evacd_a[i] = aa.n

            @block.vector
            def _(vector, i=i, ts=ts):
                for b in range(4):
                    vector.wait_ge(aa.sem, evacd_a[i] - 4 + b + 1)
                    vv.inc(vector.tensor_tensor(out=xl3[:, 4 * b:4 * b + 4, ts:ts + 128],
                                                in0=psum[b].rearrange("p (k t) -> p k t", k=4),
                                                in1=xh3[:, 4 * b:4 * b + 4, ts:ts + 128],
                                                op=ALU.subtract))
                vector.wait_ge(tt.sem, tposed[i])
                vv.inc(vector.tensor_copy(out=xq3[:, 0:8, ts:ts + 128],
                                          in_=psum[4].bitcast(BF16).rearrange("p (k t) -> p k t", k=8)))
                vv.inc(vector.tensor_copy(out=xq3[:, 8:16, ts:ts + 128],
                                          in_=psum[5].bitcast(BF16).rearrange("p (k t) -> p k t", k=8)))
            evacd_v[i] = vv.n

        @block.gpsimd
        def _(gpsimd):
            gpsimd.wait_ge(vv.sem, vv.n)
            gq.inc(gpsimd.dma_start(out=self.xq_sp, in_=xqT_sb), 16)
        self.L_done_v = vv.n
        self.L_done_g = gq.n

        if "stats" in self.DBG:
            @block.gpsimd
            def _(gpsimd):
                gpsimd.wait_ge(vv.sem, self.L_done_v)
                gq.inc(gpsimd.dma_start(out=self.DBG["stats"], in_=stat), 16)
        if "xh" in self.DBG:
            cvt = scr1
            for nm, src in (("xh", self.xh_sb), ("xl", self.xl_sb), ("xq", xqT_sb)):
                if nm not in self.DBG:
                    continue
                for j in range(KC * TOK // 2048):
                    @block.vector
                    def _(vector, j=j, src=src):
                        if j == 0:
                            vector.wait_ge(vv.sem, self.L_done_v)
                        vector.wait_ge(gq.sem, gq.n)
                        vv.inc(vector.tensor_copy(out=cvt, in_=src[:, j * 2048:(j + 1) * 2048]))

                    @block.gpsimd
                    def _(gpsimd, j=j, nm=nm):
                        gpsimd.wait_ge(vv.sem, vv.n)
                        gq.inc(gpsimd.dma_start(out=self.DBG[nm][:, j * 2048:(j + 1) * 2048], in_=cvt), 16)
        al.pop()

    def phase_W(self):
        nc, al, block = self.nc, self.al, self.block
        ld, gq, tt, vv, aa, cc = self.ld, self.gq, self.tt, self.vv, self.aa, self.cc
        stat, tiny, psum = self.stat, self.tiny, self.psum

        shbuf = self.wb["shbuf"]
        posb = self.wb["posb"]
        negb = self.wb["negb"]
        terb = self.wb["terb"]
        gsum = self.stat[:, 90:91]
        dsum = self.stat[:, 91:92]
        psum_col = self.stat[:, 92:93]
        pr2 = self.wb["pr2"]

        groups = [(g, list(range(4 * g, min(4 * g + 4, SH_CH)))) for g in range((SH_CH + 3) // 4)]

        @block.vector
        def _(vector):
            vector.memset(gsum, 0.0)
            vv.inc(vector.memset(dsum, 0.0))
        sums_ready = []
        for g, chs in groups:
            @block.sync
            def _(sync, g=g, chs=chs, gq_n=self.L_done_g):
                if g == 0:
                    sync.wait_ge(gq.sem, gq_n)
                if g >= 2:
                    sync.wait_ge(aa.sem, sums_ready[g - 2])
                ld.inc(sync.dma_start(
                    out=shbuf[g % 2][:, 0:512 * len(chs)].rearrange("p (c e) -> p c e", c=len(chs)),
                    in_=self.wshard[chs[0]:chs[0] + len(chs)].rearrange("c p e -> p c e")), 16)

            @block.scalar
            def _(scalar, g=g, chs=chs):
                scalar.wait_ge(ld.sem, self.ld.n)
                n = len(chs)
                aa.inc(scalar.activation(out=shbuf[g % 2][:, 0:512 * n], in_=shbuf[g % 2][:, 0:512 * n],
                                         func=AF.Abs, accum_out=psum_col))
            sums_ready.append(aa.n)

            @block.vector
            def _(vector, g=g, chs=chs):
                vector.wait_ge(aa.sem, sums_ready[g])
                vector.drain()
                tgt = gsum if chs[0] < SH_GCH else dsum
                vv.inc(vector.tensor_tensor(out=tgt, in0=tgt, in1=psum_col, op=ALU.add))

        @block.gpsimd
        def _(gpsimd):
            gpsimd.wait_ge(vv.sem, vv.n)
            self.cc.inc(gpsimd.tensor_reduce(out=pr2[0:1, 0:2], in_=stat[:, 90:92], axis=AX.C, op=ALU.add))

        @block.vector
        def _(vector):
            vector.wait_ge(cc.sem, cc.n)
            vector.memset(self.arbuf[:], 0.0)
            vector.drain()
            vv.inc(vector.tensor_copy(out=self.arbuf[0:1, 8:10], in_=pr2[0:1, 0:2]))

        @block.gpsimd
        def _(gpsimd):
            gpsimd.wait_ge(vv.sem, vv.n)
            gq.inc(gpsimd.dma_start(out=self.ar_in, in_=self.arbuf), 16)
            gpsimd.wait_ge(gq.sem, gq.n)
            cc.inc(gpsimd.collective_compute("AllReduce", ALU.add,
                                             replica_groups=[list(range(NCORES))],
                                             ins=[self.ar_in.opt()], outs=[self.ar_out.opt()]))
            gpsimd.wait_ge(cc.sem, cc.n)
            gq.inc(gpsimd.dma_start(out=self.arbuf, in_=self.ar_out), 16)

        @block.vector
        def _(vector):
            vector.wait_ge(gq.sem, gq.n)
            a, b = self.arbuf[0:1, 0:1], self.arbuf[0:1, 1:2]
            vector.tensor_scalar(out=a, in0=self.arbuf[0:1, 8:9], scalar1=1.0 / GATE_ELEMS,
                                 scalar2=1e-5, op0=ALU.mult, op1=ALU.max)
            vector.tensor_scalar(out=b, in0=self.arbuf[0:1, 9:10], scalar1=1.0 / DOWN_ELEMS,
                                 scalar2=1e-5, op0=ALU.mult, op1=ALU.max)
            vector.drain()
            vector.reciprocal(out=self.arbuf[0:1, 2:3], in_=a)
            vv.inc(vector.reciprocal(out=self.arbuf[0:1, 3:4], in_=b))
        if "swv" in self.DBG:
            @block.gpsimd
            def _(gpsimd):
                gpsimd.wait_ge(vv.sem, vv.n)
                gq.inc(gpsimd.dma_start(out=self.DBG["swv"], in_=self.arbuf), 16)

        @block.tensor
        def _(tensor):
            tensor.wait_ge(vv.sem, vv.n)
            tensor.wait_ge(vv.sem, self.L_done_v)
            tensor.drain()
            for j, src in enumerate([self.arbuf[0:1, 2:3], self.arbuf[0:1, 3:4],
                                     self.arbuf[0:1, 0:1], self.arbuf[0:1, 1:2]]):
                tt.inc(tensor.matmul(psum[6][:, j:j + 1], self.onesr, src, start=True, stop=True))
            tensor.drain()

        @block.vector
        def _(vector):
            vector.wait_ge(tt.sem, tt.n)
            vv.inc(vector.tensor_copy(out=tiny[:, 0:4], in_=psum[6][:, 0:4]))
        self.scales_v = vv.n

        tern_done = []
        for g, chs in groups:
            n = len(chs)

            @block.sync
            def _(sync, g=g, chs=chs, n=n):
                sync.wait_ge(vv.sem, self.scales_v)
                if g >= 2:
                    sync.wait_ge(vv.sem, tern_done[g - 2])
                ld.inc(sync.dma_start(
                    out=shbuf[g % 2][:, 0:512 * n].rearrange("p (c e) -> p c e", c=n),
                    in_=self.wshard[chs[0]:chs[0] + n].rearrange("c p e -> p c e")), 16)

            @block.vector
            def _(vector, g=g, chs=chs, n=n):
                vector.wait_ge(ld.sem, self.ld.n)
                if g >= 2:
                    vector.wait_ge(gq.sem, tern_g[g - 2])
                sc = tiny[:, 0:1] if chs[0] < SH_GCH else tiny[:, 1:2]
                src = shbuf[g % 2][:, 0:512 * n]
                vector.tensor_scalar(out=posb[:, 0:2048 * n // 4], in0=src, scalar1=sc, scalar2=0.5,
                                     op0=ALU.mult, op1=ALU.is_gt)
                vector.tensor_scalar(out=negb[:, 0:2048 * n // 4], in0=src, scalar1=sc, scalar2=-0.5,
                                     op0=ALU.mult, op1=ALU.is_lt)
                vector.drain()
                vv.inc(vector.tensor_tensor(out=terb[g % 2][:, 0:2048 * n // 4],
                                            in0=posb[:, 0:2048 * n // 4],
                                            in1=negb[:, 0:2048 * n // 4], op=ALU.subtract))
                vector.drain()
            tern_done.append(vv.n)

            if g == 0:
                tern_g = []

            @block.gpsimd
            def _(gpsimd, g=g, chs=chs, n=n):
                gpsimd.wait_ge(vv.sem, tern_done[g])
                gq.inc(gpsimd.dma_start(
                    out=self.tern_in[chs[0]:chs[0] + n].rearrange("c p e -> p c e"),
                    in_=terb[g % 2][:, 0:2048 * n // 4].rearrange("p (c e) -> p c e", c=n)), 16)
            tern_g.append(gq.n)

        @block.gpsimd
        def _(gpsimd):
            gpsimd.wait_ge(gq.sem, gq.n)
            cc.inc(gpsimd.collective_compute("AllGather", ALU.bypass,
                                             replica_groups=[list(range(NCORES))],
                                             ins=[self.tern_in.opt()], outs=[self.tern_out.opt()]))
        self.tern_cc = cc.n

        if "tern" in self.DBG:
            cvt = al.take(2048)
            me = 0
            for ch in range(SH_CH):
                @block.sync
                def _(sync, ch=ch):
                    if ch == 0:
                        sync.wait_ge(cc.sem, self.tern_cc)
                    sync.wait_ge(vv.sem, vv.n)
                    ld.inc(sync.dma_start(out=cvt[:, 0:128].bitcast(FP8), in_=self.tern_in[ch]), 16)

                @block.vector
                def _(vector, ch=ch, gq_n=gq.n):
                    vector.wait_ge(ld.sem, self.ld.n)
                    vector.wait_ge(gq.sem, gq_n)
                    vv.inc(vector.tensor_copy(out=cvt[:, 512:1024], in_=cvt[:, 0:128].bitcast(FP8)))

                @block.gpsimd
                def _(gpsimd, ch=ch):
                    gpsimd.wait_ge(vv.sem, vv.n)
                    gq.inc(gpsimd.dma_start(out=self.DBG["tern"][ch], in_=cvt[:, 512:1024]), 16)

    def phase_P(self):
        nc, al, block = self.nc, self.al, self.block
        ld, gq, tt, vv, aa = self.ld, self.gq, self.tt, self.vv, self.aa
        psum = self.psum
        xh3, xl3 = self.xh3, self.xl3

        al.push()
        wblk = [al.take(512) for _ in range(2)]
        wbh = [al.take(256, BF16) for _ in range(2)]
        wbl = [al.take(256, BF16) for _ in range(2)]
        bounce = [al.take(512) for _ in range(2)]

        prev_evac_a = aa.n
        for wname in ("k", "v", "q"):
            wt = self.w_t[wname]
            sp = self.qkv_sp[wname]
            for mp in range(4):
                split_done = []
                for k in range(KC):
                    @block.sync
                    def _(sync, mp=mp, k=k, wt=wt, vv_emit=vv.n, first=(prev_evac_a == self.aa.n and mp == 0 and k == 0)):
                        if k >= 2:
                            sync.wait_ge(vv.sem, split_done[k - 2])
                        else:
                            sync.wait_ge(vv.sem, vv_emit)
                        if first:
                            sync.wait_ge(self.gq.sem, self.L_done_g)
                        ld.inc(sync.dma_start(out=wblk[k % 2], in_=wt[mp, k]), 16)

                    tt_at_emit = tt.n

                    @block.vector
                    def _(vector, k=k, tt_at_emit=tt_at_emit):
                        vector.wait_ge(ld.sem, self.ld.n)
                        if k >= 2:
                            vector.wait_ge(tt.sem, mm_done[k - 2])
                        else:
                            vector.wait_ge(tt.sem, tt_at_emit)
                        vector.tensor_copy(out=wbh[k % 2][:, 0:512], in_=wblk[k % 2])
                        vector.drain()
                        vv.inc(vector.tensor_tensor(out=wbl[k % 2][:, 0:512], in0=wblk[k % 2],
                                                    in1=wbh[k % 2][:, 0:512], op=ALU.subtract))
                    split_done.append(vv.n)

                    if k == 0:
                        mm_done = []

                    @block.tensor
                    def _(tensor, k=k, mp=mp):
                        tensor.wait_ge(vv.sem, split_done[k])
                        if k == 0:
                            tensor.wait_ge(aa.sem, prev_evac_a)
                        h = wbh[k % 2][:, 0:512]
                        l = wbl[k % 2][:, 0:512]
                        for t in range(8):
                            xh_k = xh3[:, k, t * 128:(t + 1) * 128]
                            xl_k = xl3[:, k, t * 128:(t + 1) * 128]
                            tensor.matmul(psum[t], xh_k, h, start=(k == 0), stop=False)
                            tensor.matmul(psum[t], xh_k, l, start=False, stop=False)
                            mm = tensor.matmul(psum[t], xl_k, h, start=False, stop=(k == KC - 1))
                            if t == 7:
                                tt.inc(mm)
                    mm_done.append(tt.n)

                for t in range(8):
                    @block.scalar
                    def _(scalar, t=t, mp=mp):
                        scalar.wait_ge(tt.sem, mm_done[KC - 1])
                        if t >= 2:
                            scalar.wait_ge(gq.sem, spilled[t - 2])
                        aa.inc(scalar.activation(out=bounce[t % 2], in_=psum[t], func=AF.Copy))

                    if t == 0:
                        spilled = []

                    @block.gpsimd
                    def _(gpsimd, t=t, mp=mp, sp=sp):
                        gpsimd.wait_ge(aa.sem, self.aa.n)
                        gq.inc(gpsimd.dma_start(out=sp[t * 128:(t + 1) * 128, mp * 512:(mp + 1) * 512],
                                                in_=bounce[t % 2]), 16)
                    spilled.append(gq.n)
                prev_evac_a = aa.n

        self.P_done_g = gq.n
        if "qkv" in self.DBG:
            cvt = al.take(2048)
            for j, nm in enumerate(("q", "k", "v")):
                for r in range(NTILE):
                    @block.sync
                    def _(sync, j=j, nm=nm, r=r):
                        sync.wait_ge(gq.sem, self.P_done_g)
                        sync.wait_ge(gq.sem, self.gq.n)
                        ld.inc(sync.dma_start(out=cvt, in_=self.qkv_sp[nm][r * 128:(r + 1) * 128, :]), 16)

                    @block.gpsimd
                    def _(gpsimd, j=j, r=r):
                        gpsimd.wait_ge(ld.sem, self.ld.n)
                        gq.inc(gpsimd.dma_start(out=self.DBG["qkv"][j, r * 128:(r + 1) * 128, :], in_=cvt), 16)
        al.pop()

    def phase_A(self):
        nc, al, block = self.nc, self.al, self.block
        ld, gq, tt, vv, aa, cc = self.ld, self.gq, self.tt, self.vv, self.aa, self.cc
        psum, ident, kvst = self.psum, self.ident, self.kvst
        rC = self.rc["rC"]

        lvl = 5
        for n in (1, 2, 3, 4):
            if f"a{n}" in self.dbg:
                lvl = n
        al.push()
        dmask_s = al.take(D)
        qdec_s = al.take(D)
        kdecb_s = al.take(D)
        cosb = [al.take(512) for _ in range(2)]
        sinb = [al.take(512) for _ in range(2)]
        qc = [al.take(D) for _ in range(2)]
        kc = [al.take(D) for _ in range(2)]
        vc = [al.take(D) for _ in range(2)]
        kd = al.take(D)
        rtmp = [al.take(512) for _ in range(4)]
        qT4 = al.take(512)
        kT4 = al.take(512)
        sm4 = al.take(512)
        qdT_c = [al.take(D) for _ in range(2)]
        attnT_c = [al.take(D) for _ in range(2)]

        @block.sync
        def _(sync, gq_n=self.P_done_g, tt_n=tt.n):
            sync.wait_ge(gq.sem, gq_n)
            sync.wait_ge(tt.sem, tt_n)
            ld.inc(sync.dma_start(out=dmask_s.rearrange("p (g e) -> p g e", g=4), in_=self.dmask4_d.rearrange("g p e -> p g e")), 16)
            ld.inc(sync.dma_start(out=qdec_s.rearrange("p (g e) -> p g e", g=4), in_=self.qdec4_d.rearrange("g p e -> p g e")), 16)
            ld.inc(sync.dma_start(out=kdecb_s, in_=self.kdecb_d), 16)

        @block.vector
        def _(vector):
            vector.memset(kvst[:], 0.0)
            vv.inc(vector.memset(qT4, 0.0))
            vector.drain()
        kv_zero_v = vv.n

        chunk_ld = []
        rot_done = [0] * NCHUNK
        spill_g = [0] * NCHUNK
        evQ = evK = evS = evO = evU = 0
        kvup = [[0] * 4 for _ in range(NCHUNK)]

        for c in range(NCHUNK):
            @block.sync
            def _(sync, c=c, rd=rot_done, tt_n=tt.n):
                if c >= 2:
                    sync.wait_ge(tt.sem, grp_tt[c - 2])
                for nm, buf in (("q", qc[c % 2]), ("k", kc[c % 2]), ("v", vc[c % 2])):
                    ld.inc(sync.dma_start(out=buf, in_=self.qkv_sp[nm][c * 128:(c + 1) * 128, :]), 16)
                ld.inc(sync.dma_start(out=cosb[c % 2], in_=self.cosrep_d[c]), 16)
                ld.inc(sync.dma_start(out=sinb[c % 2], in_=self.sinrep_d[c]), 16)
            chunk_ld.append(ld.n)

            @block.vector
            def _(vector, c=c, prev_tt=(grp_tt[c - 1] if c > 0 else 0)):
                vector.wait_ge(ld.sem, chunk_ld[c])
                if c == 0:
                    vector.wait_ge(vv.sem, kv_zero_v)
                else:
                    vector.wait_ge(tt.sem, prev_tt)
                cs = cosb[c % 2].rearrange("p (h j) -> p h j", h=H)
                sn = sinb[c % 2].rearrange("p (h j) -> p h j", h=H)
                for buf in (qc[c % 2], kc[c % 2]):
                    b3 = buf.rearrange("p (h d) -> p h d", h=H)
                    x1 = b3[:, :, 0:32]
                    x2 = b3[:, :, 64:96]
                    r0 = rtmp[0].rearrange("p (h j) -> p h j", h=H)
                    r1 = rtmp[1].rearrange("p (h j) -> p h j", h=H)
                    r2 = rtmp[2].rearrange("p (h j) -> p h j", h=H)
                    r3 = rtmp[3].rearrange("p (h j) -> p h j", h=H)
                    vector.tensor_tensor(out=r0, in0=x1, in1=cs, op=ALU.mult)
                    vector.tensor_tensor(out=r1, in0=x2, in1=sn, op=ALU.mult)
                    vector.tensor_tensor(out=r2, in0=x1, in1=sn, op=ALU.mult)
                    vector.tensor_tensor(out=r3, in0=x2, in1=cs, op=ALU.mult)
                    vector.drain()
                    vector.tensor_tensor(out=x1, in0=r0, in1=r1, op=ALU.add)
                    vector.tensor_tensor(out=x2, in0=r3, in1=r2, op=ALU.subtract)
                    vector.drain()
                vv.inc(vector.tensor_tensor(out=kd, in0=kc[c % 2], in1=kdecb_s, op=ALU.mult))
            rot_done[c] = vv.n

            grp_tt = grp_tt if c > 0 else [0] * NCHUNK
            if lvl < 2:
                grp_tt[c] = tt.n
                continue
            for g in range(4):
                @block.tensor
                def _(tensor, c=c, g=g, evQ=evQ, evK=evK):
                    tensor.wait_ge(vv.sem, rot_done[c])
                    if evQ:
                        tensor.wait_ge(aa.sem, evK)
                        tensor.wait_ge(vv.sem, evQ)
                    for hq in range(4):
                        h = 4 * g + hq
                        mm = tensor.transpose(psum[0][:, hq * 128:(hq + 1) * 128],
                                              qc[c % 2][:, h * 128:(h + 1) * 128], ident)
                        if hq == 3:
                            tt.inc(mm)
                    for hq in range(4):
                        h = 4 * g + hq
                        mm = tensor.transpose(psum[1][:, hq * 128:(hq + 1) * 128],
                                              kc[c % 2][:, h * 128:(h + 1) * 128], ident)
                        if hq == 3:
                            tt.inc(mm)
                tposeQ, tposeK = tt.n - 1, tt.n

                @block.scalar
                def _(scalar, c=c, g=g, tposeQ=tposeQ, tposeK=tposeK):
                    scalar.wait_ge(tt.sem, tposeQ)
                    aa.inc(scalar.activation(out=qT4, in_=psum[0], func=AF.Copy))
                    scalar.wait_ge(tt.sem, tposeK)
                    aa.inc(scalar.activation(out=kT4, in_=psum[1], func=AF.Copy))
                evK = aa.n

                @block.vector
                def _(vector, c=c, g=g, qa=aa.n - 1, spv=spill_g[c - 2] if c >= 2 else 0):
                    vector.wait_ge(aa.sem, qa)
                    if g == 0 and spv:
                        vector.wait_ge(gq.sem, spv)
                    vv.inc(vector.tensor_tensor(out=qdT_c[c % 2][:, g * 512:(g + 1) * 512],
                                                in0=qT4, in1=qdec_s[:, g * 512:(g + 1) * 512],
                                                op=ALU.mult))
                evQ = vv.n

                if lvl < 3:
                    grp_tt[c] = tt.n
                    continue

                @block.tensor
                def _(tensor, c=c, g=g, evK=evK, evQ=evQ, evS=evS):
                    tensor.wait_ge(aa.sem, evK)
                    tensor.wait_ge(vv.sem, evQ)
                    if evS:
                        tensor.wait_ge(vv.sem, evS)
                    for hq in range(4):
                        mm = tensor.matmul(psum[2][:, hq * 128:(hq + 1) * 128],
                                           kT4[:, hq * 128:(hq + 1) * 128],
                                           qT4[:, hq * 128:(hq + 1) * 128],
                                           start=True, stop=True)
                        if hq == 3:
                            tt.inc(mm)
                sc_tt = tt.n

                @block.vector
                def _(vector, c=c, g=g, sc_tt=sc_tt):
                    vector.wait_ge(tt.sem, sc_tt)
                    vv.inc(vector.tensor_tensor(out=sm4, in0=psum[2],
                                                in1=dmask_s[:, g * 512:(g + 1) * 512], op=ALU.mult))
                evS = vv.n

                if lvl < 4:
                    grp_tt[c] = tt.n
                    kvup[c][g] = vv.n
                    continue

                @block.tensor
                def _(tensor, c=c, g=g, evS=evS, evO=evO, evU=evU,
                      kv_prev=(kvup[c - 1][g] if c > 0 else kv_zero_v)):
                    tensor.wait_ge(vv.sem, evS)
                    tensor.wait_ge(vv.sem, kv_prev)
                    if evO:
                        tensor.wait_ge(aa.sem, evO)
                    if evU:
                        tensor.wait_ge(vv.sem, evU)
                    for hq in range(4):
                        h = 4 * g + hq
                        tensor.matmul(psum[3][:, hq * 128:(hq + 1) * 128],
                                      vc[c % 2][:, h * 128:(h + 1) * 128],
                                      sm4[:, hq * 128:(hq + 1) * 128], start=True, stop=False)
                        mm = tensor.matmul(psum[3][:, hq * 128:(hq + 1) * 128],
                                           kvst[:, h * 128:(h + 1) * 128],
                                           qdT_c[c % 2][:, (g * 512 + hq * 128):(g * 512 + (hq + 1) * 128)],
                                           start=False, stop=True, skip_group_check=True)
                        if hq == 3:
                            tt.inc(mm)
                    for hq in range(4):
                        h = 4 * g + hq
                        mm = tensor.matmul(psum[4][:, hq * 128:(hq + 1) * 128],
                                           kd[:, h * 128:(h + 1) * 128],
                                           vc[c % 2][:, h * 128:(h + 1) * 128], start=True, stop=True)
                        if hq == 3:
                            tt.inc(mm)
                out_tt, u_tt = tt.n - 1, tt.n
                grp_tt[c] = tt.n

                @block.scalar
                def _(scalar, c=c, g=g, out_tt=out_tt, spa=(spill_g[c - 2] if c >= 2 else 0)):
                    scalar.wait_ge(tt.sem, out_tt)
                    if g == 0 and spa:
                        scalar.wait_ge(gq.sem, spa)
                    aa.inc(scalar.activation(out=attnT_c[c % 2][:, g * 512:(g + 1) * 512],
                                             in_=psum[3], func=AF.Copy))
                evO = aa.n

                if lvl < 5:
                    kvup[c][g] = vv.n
                    continue

                @block.vector
                def _(vector, c=c, g=g, u_tt=u_tt):
                    vector.wait_ge(tt.sem, u_tt)
                    for hq in range(4):
                        h = 4 * g + hq
                        mm = vector.scalar_tensor_tensor(
                            out=kvst[:, h * 128:(h + 1) * 128],
                            in0=kvst[:, h * 128:(h + 1) * 128],
                            scalar=float(rC[h]),
                            in1=psum[4][:, hq * 128:(hq + 1) * 128],
                            op0=ALU.mult, op1=ALU.add)
                        if hq == 3:
                            vv.inc(mm)
                    vector.drain()
                evU = vv.n
                kvup[c][g] = vv.n

            if lvl >= 4:
                @block.gpsimd
                def _(gpsimd, c=c, aan=aa.n, vvn=vv.n):
                    gpsimd.wait_ge(aa.sem, aan)
                    gpsimd.wait_ge(vv.sem, vvn)
                    gq.inc(gpsimd.dma_start(out=self.attn_sp[c], in_=attnT_c[c % 2]), 16)
                    gq.inc(gpsimd.dma_start(out=self.qd_sp[c], in_=qdT_c[c % 2]), 16)
            spill_g[c] = gq.n

        if lvl < 5 or "noz" in self.dbg:
            self.z_cc = cc.n
            if "attn" in self.DBG and lvl >= 4:
                for c in range(NCHUNK):
                    @block.sync
                    def _(sync, c=c, gqn=gq.n):
                        sync.wait_ge(gq.sem, gqn)
                        ld.inc(sync.dma_start(out=qc[0], in_=self.attn_sp[c]), 16)

                    @block.gpsimd
                    def _(gpsimd, c=c, ldn=ld.n):
                        gpsimd.wait_ge(ld.sem, ldn)
                        gq.inc(gpsimd.dma_start(out=self.DBG["attn"][c], in_=qc[0]), 16)
            al.pop()
            return

        @block.gpsimd
        def _(gpsimd, vvn=vv.n):
            gpsimd.wait_ge(vv.sem, vvn)
            gq.inc(gpsimd.dma_start(out=self.z_in, in_=kvst[:]), 16)
            gpsimd.wait_ge(gq.sem, gq.n)
            cc.inc(gpsimd.collective_compute("AllGather", ALU.bypass,
                                             replica_groups=[list(range(NCORES))],
                                             ins=[self.z_in.opt()], outs=[self.z_out.opt()]))
        self.z_cc = cc.n

        if "attn" in self.DBG:
            for c in range(NCHUNK):
                @block.sync
                def _(sync, c=c, gqn=gq.n, zcc=self.z_cc):
                    sync.wait_ge(gq.sem, gqn)
                    sync.wait_ge(cc.sem, zcc)
                    ld.inc(sync.dma_start(out=qc[0], in_=self.attn_sp[c]), 16)

                @block.gpsimd
                def _(gpsimd, c=c, ldn=ld.n):
                    gpsimd.wait_ge(ld.sem, ldn)
                    gq.inc(gpsimd.dma_start(out=self.DBG["attn"][c], in_=qc[0]), 16)
        if "z" in self.DBG:
            @block.gpsimd
            def _(gpsimd, ccn=cc.n):
                gpsimd.wait_ge(cc.sem, ccn)
                for r in range(NCORES):
                    gq.inc(gpsimd.dma_start(out=self.DBG["z"][r * 128:(r + 1) * 128, :],
                                            in_=self.z_out[r * 128:(r + 1) * 128, :]), 16)
        al.pop()

    def phase_C(self):
        nc, al, block = self.nc, self.al, self.block
        ld, gq, tt, vv, aa, cc = self.ld, self.gq, self.tt, self.vv, self.aa, self.cc
        psum, ident, kvst, stat = self.psum, self.ident, self.kvst, self.stat
        rc_pow = self.rc["rc_pow"]

        al.push()
        zbuf = [al.take(D) for _ in range(2)]
        zw_s = al.take(128)
        at_b = [al.take(D) for _ in range(2)]
        qd_b = [al.take(D) for _ in range(2)]
        qs4 = al.take(512)
        sqt = al.take(D)
        rowst = al.take(2048)
        onescol = al.take(128)
        ah_sb = al.take(NCHUNK * D // 2, BF16)
        al_sb = al.take(NCHUNK * D // 2, BF16)
        self.py_done_g = 0

        @block.sync
        def _(sync, ccn=self.z_cc):
            sync.wait_ge(cc.sem, ccn)
            ld.inc(sync.dma_start(out=zw_s[:, 0:128], in_=self.zw_d), 16)
            ld.inc(sync.dma_start(out=zbuf[0], in_=self.z_out[0:128, :]), 16)
            ld.inc(sync.dma_start(out=zbuf[1], in_=self.z_out[128:256, :]), 16)

        @block.vector
        def _(vector, ldn=ld.n):
            vector.wait_ge(ld.sem, ldn - 32)
            vector.memset(kvst[:], 0.0)
            vv.inc(vector.drain() if False else vector.memset(rowst, 0.0))
            vector.drain()
        zl = [ld.n - 16, ld.n]
        zdone = []
        for cp in range(NCORES):
            @block.vector
            def _(vector, cp=cp, zl0=zl[cp]):
                vector.wait_ge(ld.sem, zl0)
                for h in range(H):
                    mm = vector.scalar_tensor_tensor(
                        out=kvst[:, h * 128:(h + 1) * 128],
                        in0=zbuf[cp % 2][:, h * 128:(h + 1) * 128],
                        scalar=zw_s[:, cp * 16 + h:cp * 16 + h + 1],
                        in1=kvst[:, h * 128:(h + 1) * 128],
                        op0=ALU.mult, op1=ALU.add)
                    if h == H - 1:
                        vv.inc(mm)
                vector.drain()
            zdone.append(vv.n)
            if cp + 2 < NCORES:
                @block.sync
                def _(sync, cp=cp):
                    sync.wait_ge(vv.sem, zdone[cp])
                    ld.inc(sync.dma_start(out=zbuf[cp % 2], in_=self.z_out[(cp + 2) * 128:(cp + 3) * 128, :]), 16)
                zl.append(ld.n)
        sstart_v = vv.n

        atl = []
        rms_a = []
        rms_sq = []
        for c in range(NCHUNK):
            @block.sync
            def _(sync, c=c, prev=(atl[c - 2] if c >= 2 else 0),
                  prev_sq=(rms_sq[c - 2] if c >= 2 else 0)):
                if c >= 2:
                    sync.wait_ge(vv.sem, prev)
                    sync.wait_ge(aa.sem, prev_sq)
                ld.inc(sync.dma_start(out=qd_b[c % 2], in_=self.qd_sp[c]), 16)
                ld.inc(sync.dma_start(out=at_b[c % 2], in_=self.attn_sp[c]), 16)
            this_ld = ld.n

            corr_tt = []
            for g in range(4):
                @block.vector
                def _(vector, c=c, g=g, this_ld=this_ld, ctt=(corr_tt[g - 1] if g else 0)):
                    vector.wait_ge(ld.sem, this_ld)
                    if g > 0:
                        vector.wait_ge(tt.sem, ctt)
                    for hq in range(4):
                        h = 4 * g + hq
                        mm = vector.tensor_scalar(
                            out=qs4[:, hq * 128:(hq + 1) * 128],
                            in0=qd_b[c % 2][:, (g * 512 + hq * 128):(g * 512 + (hq + 1) * 128)],
                            scalar1=float(rc_pow[h][c]), scalar2=None, op0=ALU.mult)
                        if hq == 3:
                            vv.inc(mm)
                qs_v = vv.n

                @block.tensor
                def _(tensor, c=c, g=g, qs_v=qs_v, sstart_v=sstart_v, pv=0):
                    tensor.wait_ge(vv.sem, qs_v)
                    tensor.wait_ge(vv.sem, sstart_v)
                    for hq in range(4):
                        h = 4 * g + hq
                        mm = tensor.matmul(psum[5][:, hq * 128:(hq + 1) * 128],
                                           kvst[:, h * 128:(h + 1) * 128],
                                           qs4[:, hq * 128:(hq + 1) * 128], start=True, stop=True)
                        if hq == 3:
                            tt.inc(mm)
                corr_tt.append(tt.n)

                @block.vector
                def _(vector, c=c, g=g, ctt=corr_tt[g]):
                    vector.wait_ge(tt.sem, ctt)
                    vv.inc(vector.tensor_tensor(out=at_b[c % 2][:, g * 512:(g + 1) * 512],
                                                in0=at_b[c % 2][:, g * 512:(g + 1) * 512],
                                                in1=psum[5], op=ALU.add))
                    vector.drain()
            corr_v = vv.n

            @block.scalar
            def _(scalar, c=c, corr_v=corr_v):
                scalar.wait_ge(vv.sem, corr_v)
                aa.inc(scalar.activation(out=sqt, in_=at_b[c % 2], func=AF.Square))
            sq_a = aa.n
            rms_sq.append(sq_a)

            @block.vector
            def _(vector, c=c):
                if c == 0:
                    vector.memset(onescol[:, 0:1], 1.0)
                    vv.inc(vector.drain() if False else vector.memset(rowst[0:1, 1900:1901], 0.0))
                    vector.drain()
            ones_v = vv.n

            @block.tensor
            def _(tensor, c=c, sq_a=sq_a, ones_v=ones_v, prev_red=(rms_a[c - 1] if c else 0)):
                tensor.wait_ge(aa.sem, sq_a)
                tensor.wait_ge(vv.sem, ones_v)
                if prev_red:
                    tensor.wait_ge(vv.sem, prev_red)
                for j in range(4):
                    mm = tensor.matmul(psum[4 + j][0:1, 0:512], onescol[:, 0:1],
                                       sqt[:, j * 512:(j + 1) * 512], start=True, stop=True)
                    if j == 3:
                        tt.inc(mm)
            red_tt = tt.n

            @block.vector
            def _(vector, c=c, red_tt=red_tt):
                vector.wait_ge(tt.sem, red_tt)
                for j in range(4):
                    vector.tensor_reduce(out=rowst[0:1, 1024 + j * 128:1024 + (j + 1) * 128],
                                         in_=psum[4 + j][0:1, :].rearrange("p (h t) -> p t h", h=4),
                                         axis=AX.X, op=ALU.add)
                vector.drain()
                vector.tensor_tensor(out=rowst[0:1, 1024:1152], in0=rowst[0:1, 1024:1152],
                                     in1=rowst[0:1, 1152:1280], op=ALU.add)
                vector.tensor_tensor(out=rowst[0:1, 1280:1408], in0=rowst[0:1, 1280:1408],
                                     in1=rowst[0:1, 1408:1536], op=ALU.add)
                vector.drain()
                vector.tensor_tensor(out=rowst[0:1, 1024:1152], in0=rowst[0:1, 1024:1152],
                                     in1=rowst[0:1, 1280:1408], op=ALU.add)
                vector.drain()
                vv.inc(vector.tensor_scalar(out=rowst[0:1, c * 128:(c + 1) * 128],
                                            in0=rowst[0:1, 1024:1152],
                                            scalar1=INV_D, scalar2=EPS, op0=ALU.mult, op1=ALU.add))
                vector.drain()
            rms_a.append(vv.n)

            @block.vector
            def _(vector, c=c):
                ah = ah_sb.rearrange("p (c e) -> p c e", c=NCHUNK)[:, c, :]
                alo = al_sb.rearrange("p (c e) -> p c e", c=NCHUNK)[:, c, :]
                vector.tensor_copy(out=ah, in_=at_b[c % 2])
                vector.drain()
                vv.inc(vector.tensor_tensor(out=alo, in0=at_b[c % 2], in1=ah, op=ALU.subtract))
                vector.drain()
            atl.append(vv.n)

            if "attnc" in self.DBG:
                @block.gpsimd
                def _(gpsimd, c=c, vvn=vv.n):
                    gpsimd.wait_ge(vv.sem, vvn)
                    gq.inc(gpsimd.dma_start(out=self.DBG["attnc"][c], in_=at_b[c % 2]), 16)

        @block.scalar
        def _(scalar, rv=rms_a[-1]):
            scalar.wait_ge(vv.sem, rv)
            aa.inc(scalar.activation(out=rowst[0:1, 1024:1024 + 1024], in_=rowst[0:1, 0:1024], func=AF.Sqrt))
        sq2_a = aa.n

        @block.vector
        def _(vector, sq2_a=sq2_a):
            vector.wait_ge(aa.sem, sq2_a)
            vv.inc(vector.reciprocal(out=rowst[0:1, 0:1024], in_=rowst[0:1, 1024:2048]))
            vector.drain()
        rr_v = vv.n

        @block.tensor
        def _(tensor, rr_v=rr_v):
            tensor.wait_ge(vv.sem, rr_v)
            tensor.drain()
            for c in range(NCHUNK):
                mm = tensor.matmul(psum[7][:, c:c + 1], rowst[0:1, c * 128:(c + 1) * 128],
                                   onescol[0:1, 0:1], start=True, stop=True, skip_group_check=True)
                if c == NCHUNK - 1:
                    tt.inc(mm)
            tensor.drain()
        rrT_tt = tt.n

        @block.vector
        def _(vector, rrT_tt=rrT_tt):
            vector.wait_ge(tt.sem, rrT_tt)
            vv.inc(vector.tensor_copy(out=self.rrv[:, 0:NCHUNK], in_=psum[7][:, 0:NCHUNK]))
        rr_ready = vv.n
        if "rr" in self.DBG:
            @block.gpsimd
            def _(gpsimd, rr_ready=rr_ready):
                gpsimd.wait_ge(vv.sem, rr_ready)
                gq.inc(gpsimd.dma_start(out=self.DBG["rr"][:, 0:NCHUNK], in_=self.rrv[:, 0:NCHUNK]), 16)
                gq.inc(gpsimd.dma_start(out=self.DBG["rrow"], in_=rowst[0:1, 0:2048]), 16)

        self.py_sp = nc.dram_tensor("py_sp", [TOK, D], F32).ap()
        wblk = [al.take(512) for _ in range(2)]
        wbh = [al.take(256, BF16) for _ in range(2)]
        wbl = [al.take(256, BF16) for _ in range(2)]
        bounce = [al.take(512) for _ in range(2)]
        xb2 = [al.take(D) for _ in range(2)]
        rms_s = al.take(KC)

        @block.sync
        def _(sync, rr_ready=rr_ready):
            sync.wait_ge(vv.sem, rr_ready)
            ld.inc(sync.dma_start(out=rms_s[:, 0:KC], in_=self.rms_pp_d), 16)
            ld.inc(sync.dma_start(out=xb2[0], in_=self.xin[0:128, :]), 16)
            ld.inc(sync.dma_start(out=xb2[1], in_=self.xin[128:256, :]), 16)
        x_ld = [ld.n - 16, ld.n]

        prev_evac_v = 0
        ah3 = ah_sb.rearrange("p (c e) -> p c e", c=NCHUNK)
        al3 = al_sb.rearrange("p (c e) -> p c e", c=NCHUNK)
        spilled_py = []
        for mp in range(4):
            split_done = []
            mm_done = []
            for k in range(KC):
                @block.sync
                def _(sync, mp=mp, k=k, vv_emit=vv.n):
                    if k >= 2:
                        sync.wait_ge(vv.sem, split_done[k - 2])
                    else:
                        sync.wait_ge(vv.sem, vv_emit)
                    ld.inc(sync.dma_start(out=wblk[k % 2], in_=self.w_t["o"][mp, k]), 16)

                @block.vector
                def _(vector, k=k, tt_at_emit=tt.n, ldn=ld.n):
                    vector.wait_ge(ld.sem, ldn)
                    if k >= 2:
                        vector.wait_ge(tt.sem, mm_done[k - 2])
                    else:
                        vector.wait_ge(tt.sem, tt_at_emit)
                    vector.tensor_scalar(out=wblk[k % 2], in0=wblk[k % 2],
                                         scalar1=rms_s[:, k:k + 1], scalar2=None, op0=ALU.mult)
                    vector.drain()
                    vector.tensor_copy(out=wbh[k % 2][:, 0:512], in_=wblk[k % 2])
                    vector.drain()
                    vv.inc(vector.tensor_tensor(out=wbl[k % 2][:, 0:512], in0=wblk[k % 2],
                                                in1=wbh[k % 2][:, 0:512], op=ALU.subtract))
                split_done.append(vv.n)

                @block.tensor
                def _(tensor, mp=mp, k=k, sd=split_done[k], pe=prev_evac_v, atv=rr_ready):
                    tensor.wait_ge(vv.sem, sd)
                    if k == 0:
                        tensor.wait_ge(vv.sem, atv)
                        if pe:
                            tensor.wait_ge(vv.sem, pe)
                    h = wbh[k % 2][:, 0:512]
                    l = wbl[k % 2][:, 0:512]
                    for t in range(8):
                        ah_k = ah3[:, t, k * 128:(k + 1) * 128]
                        al_k = al3[:, t, k * 128:(k + 1) * 128]
                        tensor.matmul(psum[t], ah_k, h, start=(k == 0), stop=False)
                        tensor.matmul(psum[t], ah_k, l, start=False, stop=False)
                        mm = tensor.matmul(psum[t], al_k, h, start=False, stop=(k == KC - 1))
                        if t == 7:
                            tt.inc(mm)
                mm_done.append(tt.n)

            for t in range(8):
                @block.scalar
                def _(scalar, t=t, mp=mp, sp=(spilled_py[-2] if len(spilled_py) >= 2 else 0)):
                    if sp:
                        scalar.wait_ge(gq.sem, sp)
                    self.q2.inc(scalar.dma_start(out=xb2[t % 2],
                                                 in_=self.xin[t * 128:(t + 1) * 128, :]), 16)

                @block.vector
                def _(vector, t=t, mp=mp, mmd=mm_done[KC - 1], ldn=self.q2.n,
                      spg=(spilled_py[-2] if len(spilled_py) >= 2 else 0)):
                    vector.wait_ge(tt.sem, mmd)
                    vector.wait_ge(self.q2.sem, ldn)
                    if spg:
                        vector.wait_ge(gq.sem, spg)
                    if "xbdump" in self.dbg:
                        vector.tensor_scalar(out=bounce[t % 2],
                                             in0=xb2[t % 2][:, mp * 512:(mp + 1) * 512],
                                             scalar1=1.0, scalar2=None, op0=ALU.mult)
                    else:
                        vector.tensor_scalar(out=bounce[t % 2], in0=psum[t],
                                             scalar1=self.rrv[:, t:t + 1], scalar2=None,
                                             op0=ALU.mult)
                    if "nox" in self.dbg:
                        vv.inc(vector.drain() if False else vector.tensor_scalar(
                            out=bounce[t % 2], in0=bounce[t % 2], scalar1=1.0, scalar2=None, op0=ALU.mult))
                    else:
                        vector.drain()
                        vv.inc(vector.tensor_tensor(out=bounce[t % 2], in0=bounce[t % 2],
                                                    in1=xb2[t % 2][:, mp * 512:(mp + 1) * 512], op=ALU.add))
                ev_v = vv.n

                @block.gpsimd
                def _(gpsimd, t=t, mp=mp, ev_v=ev_v):
                    gpsimd.wait_ge(vv.sem, ev_v)
                    gq.inc(gpsimd.dma_start(out=self.py_sp[t * 128:(t + 1) * 128, mp * 512:(mp + 1) * 512],
                                            in_=bounce[t % 2]), 16)
                spilled_py.append(gq.n)
            prev_evac_v = vv.n

        self.py_done_g = gq.n
        if "py" in self.DBG:
            for t in range(NTILE):
                @block.sync
                def _(sync, t=t, gqn=gq.n):
                    sync.wait_ge(gq.sem, gqn)
                    ld.inc(sync.dma_start(out=xb2[0], in_=self.py_sp[t * 128:(t + 1) * 128, :]), 16)

                @block.gpsimd
                def _(gpsimd, t=t, ldn=ld.n):
                    gpsimd.wait_ge(ld.sem, ldn)
                    gq.inc(gpsimd.dma_start(out=self.DBG["py"][t * 128:(t + 1) * 128, :], in_=xb2[0]), 16)
        al.pop()

    def phase_F(self):
        nc, al, block = self.nc, self.al, self.block
        ld, gq, tt, vv, aa, cc = self.ld, self.gq, self.tt, self.vv, self.aa, self.cc
        psum, stat, tiny, ident = self.psum, self.stat, self.tiny, self.ident

        al.push()
        xqs_f = al.take(2048)
        xqs = xqs_f.bitcast(BF16)
        pyb = [al.take(D) for _ in range(2)]
        hid = [al.take(FF) for _ in range(2)]
        g8 = [al.take(2048, FP8) for _ in range(2)]
        u8 = [al.take(2048, FP8) for _ in range(2)]
        d8 = [al.take(DOWN_KC * 128, FP8) for _ in range(2)]
        rg_sb = al.take(512)
        uu_sb = al.take(512)
        sq_sb = al.take(512)
        q3f = al.take(512)
        q3i = al.take(512, I32)
        q3b = al.take(256, BF16)
        xq3T = al.take(DOWN_KC * 256 // 2, BF16)
        ybounce = [al.take(512) for _ in range(2)]
        mxall = al.take(32)
        invg = stat[:, 72:80]
        invd_t = stat[:, 80:88]
        s3c = stat[:, 88:89]
        i3c = stat[:, 89:90]

        xq3v = xq3T.rearrange("p (k t) -> p k t", k=DOWN_KC)
        xqv = xqs.rearrange("p (k t) -> p k t", k=KC)

        def gate_rows(cg):
            r, idx = cg // SH_GCH, cg % SH_GCH
            return 66 * r + idx

        def down_rows(cd):
            r, idx = cd // SH_DCH, cd % SH_DCH
            return 66 * r + 44 + idx

        @block.vector
        def _(vector, pyg=self.py_done_g):
            vector.wait_ge(gq.sem, pyg)
            vector.tensor_scalar(out=invg, in0=stat[:, 64:72], scalar1=tiny[:, 2:3],
                                 scalar2=None, op0=ALU.mult)
            vv.inc(vector.drain() if False else vector.tensor_copy(out=stat[:, 80:88], in_=invg))
            vector.drain()
        scales_f = vv.n

        last_down_tt = 0
        ystored = []
        yevac = []
        for s in range(FFN_SLABS):
            @block.sync
            def _(sync, s=s, sf=scales_f, ldt=last_down_tt):
                sync.wait_ge(vv.sem, sf)
                if ldt:
                    sync.wait_ge(tt.sem, ldt)
                sync.dma_start(out=xqs_f.rearrange("p (k t) -> p k t", k=KC),
                               in_=self.xq_sp.rearrange("p (k t) -> p k t", k=KC)[:, :, s * 256:(s + 1) * 256].bitcast(F32)).then_inc(ld.sem, 16)
                ld.n += 16
                for j in range(2):
                    ld.inc(sync.dma_start(out=pyb[j], in_=self.py_sp[(2 * s + j) * 128:(2 * s + j + 1) * 128, :]), 16)
            slab_ld = ld.n

            hid_v = []
            pair_tt = []
            for jp in range(11):
                gate_cg0 = jp * 16
                up_cg0 = (jp + 11) * 16

                @block.sync
                def _(sync, s=s, jp=jp, slab_ld=slab_ld,
                      hv=(hid_v[-2] if len(hid_v) >= 2 else 0), tcc=self.tern_cc):
                    if s == 0 and jp == 0:
                        sync.wait_ge(cc.sem, tcc)
                    if jp >= 2 and hv:
                        sync.wait_ge(vv.sem, hv)
                    for base, buf in ((gate_cg0, g8[jp % 2]), (up_cg0, u8[jp % 2])):
                        rows = [gate_rows(base + k) for k in range(16)]
                        runs = []
                        st0 = 0
                        for k in range(1, 17):
                            if k == 16 or rows[k] != rows[k - 1] + 1:
                                runs.append((st0, k))
                                st0 = k
                        off = 0
                        for (a, b) in runs:
                            n = b - a
                            sync.dma_start(
                                out=buf[:, off * 512:(off + n) * 512].bitcast(F32).rearrange(
                                    "p (c e) -> p c e", c=n),
                                in_=self.tern_out[rows[a]:rows[a] + n].bitcast(F32).rearrange(
                                    "c p e -> p c e")).then_inc(ld.sem, 16)
                            ld.n += 16
                            off += n
                pair_ld = ld.n

                for t in range(2):
                    @block.tensor
                    def _(tensor, s=s, jp=jp, t=t, pair_ld=pair_ld, slab_ld=slab_ld,
                          hvv=(hid_v[-1] if (jp > 0 or t > 0) and hid_v else 0), jp0=(jp == 0 and t == 0)):
                        tensor.wait_ge(ld.sem, pair_ld)
                        if jp0:
                            tensor.wait_ge(ld.sem, slab_ld)
                        if hvv:
                            tensor.wait_ge(vv.sem, hvv)
                        for k in range(KC):
                            lhs = xqv[:, k, t * 128:(t + 1) * 128]
                            tensor.matmul(psum[2 * t], lhs,
                                          g8[jp % 2][:, k * 512:(k + 1) * 512],
                                          start=(k == 0), stop=(k == KC - 1))
                        for k in range(KC):
                            lhs = xqv[:, k, t * 128:(t + 1) * 128]
                            mm = tensor.matmul(psum[2 * t + 1], lhs,
                                               u8[jp % 2][:, k * 512:(k + 1) * 512],
                                               start=(k == 0), stop=(k == KC - 1))
                            if k == KC - 1:
                                tt.inc(mm)
                    gu_tt = tt.n
                    pair_tt.append(gu_tt)

                    @block.scalar
                    def _(scalar, s=s, t=t, gu_tt=gu_tt):
                        scalar.wait_ge(tt.sem, gu_tt)
                        tile = 2 * s + t
                        aa.inc(scalar.activation(out=rg_sb, in_=psum[2 * t], func=AF.Relu,
                                                 scale=invg[:, tile:tile + 1]))
                    gu_aa = aa.n

                    @block.vector
                    def _(vector, s=s, jp=jp, t=t, gu_aa=gu_aa, gu_tt=gu_tt):
                        vector.wait_ge(aa.sem, gu_aa)
                        vector.wait_ge(tt.sem, gu_tt)
                        tile = 2 * s + t
                        vector.tensor_scalar(out=uu_sb, in0=psum[2 * t + 1],
                                             scalar1=invg[:, tile:tile + 1], scalar2=None, op0=ALU.mult)
                        vector.tensor_tensor(out=sq_sb, in0=rg_sb, in1=rg_sb, op=ALU.mult)
                        vector.drain()
                        vv.inc(vector.tensor_tensor(out=hid[t][:, jp * 512:(jp + 1) * 512],
                                                    in0=sq_sb, in1=uu_sb, op=ALU.mult))
                        vector.drain()
                    hid_v.append(vv.n)

            xq3_v = []
            for t in range(2):
                @block.vector
                def _(vector, s=s, t=t, hv=hid_v[-1]):
                    vector.wait_ge(vv.sem, hv)
                    t_d, t_e = stat[:, 88:89], stat[:, 89:90]
                    for p in range(11):
                        vector.tensor_scalar(out=sq_sb, in0=hid[t][:, p * 512:(p + 1) * 512],
                                             scalar1=1.0, scalar2=0.0,
                                             op0=ALU.mult, op1=ALU.max, accum_out=mxall[:, p:p + 1])
                        vector.tensor_scalar(out=sq_sb, in0=hid[t][:, p * 512:(p + 1) * 512],
                                             scalar1=-1.0, scalar2=0.0,
                                             op0=ALU.mult, op1=ALU.max, accum_out=mxall[:, 11 + p:12 + p])
                    vector.drain()
                    vector.tensor_reduce(out=t_d, in_=mxall[:, 0:22], axis=AX.X, op=ALU.max)
                    vector.drain()
                    vector.tensor_scalar(out=t_d, in0=t_d, scalar1=1e-5, scalar2=None, op0=ALU.max)
                    vector.drain()
                    vector.reciprocal(out=t_e, in_=t_d)
                    vector.drain()
                    vector.tensor_scalar(out=stat[:, 92 + t:93 + t], in0=t_e, scalar1=127.0,
                                         scalar2=None, op0=ALU.mult)
                    vector.tensor_scalar(out=t_d, in0=t_d, scalar1=1.0 / 127.0, scalar2=None, op0=ALU.mult)
                    vector.drain()
                    vv.inc(vector.tensor_tensor(out=stat[:, 94 + t:95 + t], in0=t_d, in1=tiny[:, 3:4], op=ALU.mult))
                    vector.drain()
                q_v = vv.n

                for piece in range(11):
                    @block.vector
                    def _(vector, t=t, piece=piece, q_v=q_v, ptt=(tt.n if piece >= 1 else 0)):
                        if piece == 0:
                            vector.wait_ge(vv.sem, q_v)
                        if ptt:
                            vector.wait_ge(tt.sem, ptt)
                        vector.tensor_scalar(out=q3f, in0=hid[t][:, piece * 512:(piece + 1) * 512],
                                             scalar1=stat[:, 92 + t:93 + t], scalar2=None, op0=ALU.mult)
                        vector.drain()
                        vector.tensor_copy(out=q3i, in_=q3f)
                        vector.drain()
                        vv.inc(vector.tensor_copy(out=q3b[:, 0:512], in_=q3i))
                    p_v = vv.n

                    @block.tensor
                    def _(tensor, t=t, piece=piece, p_v=p_v, pev=(vv.n if piece >= 1 else 0)):
                        tensor.wait_ge(vv.sem, p_v)
                        for j in range(4):
                            k2 = piece * 4 + j
                            mm = tensor.transpose(psum[4 + (piece % 2)][:, j * 64:(j + 1) * 64].bitcast(BF16),
                                                  q3b[:, j * 128:(j + 1) * 128], self.ident_bf[:, 0:128])
                            if j == 3:
                                tt.inc(mm)
                    tp_tt = tt.n

                    @block.vector
                    def _(vector, t=t, piece=piece, tp_tt=tp_tt):
                        vector.wait_ge(tt.sem, tp_tt)
                        vv.inc(vector.tensor_copy(
                            out=xq3v[:, piece * 4:piece * 4 + 4, t * 128:(t + 1) * 128],
                            in_=psum[4 + (piece % 2)].bitcast(BF16).rearrange("p (k e) -> p k e", k=8)[:, 0:4, :]))
                xq3_v.append(vv.n)

            down_tts = []
            for op2 in range(4):
                @block.sync
                def _(sync, s=s, op2=op2, xv=xq3_v[-1],
                      dprev=(down_tts[op2 - 2] if op2 >= 2 else (last_down_tt if op2 < 2 else 0))):
                    sync.wait_ge(vv.sem, xv)
                    if dprev:
                        sync.wait_ge(tt.sem, dprev)
                    rows0 = down_rows(op2 * DOWN_KC)
                    for half in range(2):
                        r0 = down_rows(op2 * DOWN_KC + half * 22)
                        sync.dma_start(
                            out=d8[op2 % 2][:, half * 22 * 512:(half + 1) * 22 * 512].bitcast(F32).rearrange(
                                "p (c e) -> p c e", c=22),
                            in_=self.tern_out[r0:r0 + 22].bitcast(F32).rearrange("c p e -> p c e")).then_inc(ld.sem, 16)
                        ld.n += 16
                down_ld = ld.n

                for t in range(2):
                    @block.tensor
                    def _(tensor, s=s, op2=op2, t=t, down_ld=down_ld,
                          yv=(yevac[-2] if len(yevac) >= 2 else 0)):
                        tensor.wait_ge(ld.sem, down_ld)
                        if yv:
                            tensor.wait_ge(vv.sem, yv)
                        for k2 in range(DOWN_KC):
                            mm = tensor.matmul(psum[6 + t], xq3v[:, k2, t * 128:(t + 1) * 128],
                                               d8[op2 % 2][:, k2 * 512:(k2 + 1) * 512],
                                               start=(k2 == 0), stop=(k2 == DOWN_KC - 1))
                            if k2 == DOWN_KC - 1:
                                tt.inc(mm)
                    dn_tt = tt.n
                    last_down_tt = dn_tt
                    if t == 1:
                        down_tts.append(dn_tt)

                    @block.vector
                    def _(vector, s=s, op2=op2, t=t, dn_tt=dn_tt,
                          ysg=(ystored[-2] if len(ystored) >= 2 else 0)):
                        vector.wait_ge(tt.sem, dn_tt)
                        if ysg:
                            vector.wait_ge(self.st.sem, ysg)
                        vv.inc(vector.scalar_tensor_tensor(
                            out=ybounce[t % 2], in0=psum[6 + t], scalar=stat[:, 94 + t:95 + t],
                            in1=pyb[t][:, op2 * 512:(op2 + 1) * 512], op0=ALU.mult, op1=ALU.add))
                    y_v = vv.n
                    yevac.append(y_v)

                    @block.gpsimd
                    def _(gpsimd, s=s, op2=op2, t=t, y_v=y_v):
                        gpsimd.wait_ge(vv.sem, y_v)
                        self.st.inc(gpsimd.dma_start(
                            out=self.yout[(2 * s + t) * 128:(2 * s + t + 1) * 128,
                                          op2 * 512:(op2 + 1) * 512], in_=ybounce[t % 2]), 16)
                    ystored.append(self.st.n)
        al.pop()

    def stub_out(self):
        al, block = self.al, self.block
        ld, vv = self.ld, self.vv
        al.push()
        buf = [al.take(D) for _ in range(2)]
        stored = []
        for i in range(NTILE):
            @block.sync
            def _(sync, i=i, ccn=self.cc.n):
                if i < 2:
                    sync.wait_ge(self.gq.sem, self.gq.n)
                    sync.wait_ge(self.cc.sem, ccn)
                else:
                    sync.wait_ge(self.st.sem, stored[i - 2])
                ld.inc(sync.dma_start(out=buf[i % 2], in_=self.xin[i * 128:(i + 1) * 128, :]), 16)

            @block.gpsimd
            def _(gpsimd, i=i):
                gpsimd.wait_ge(ld.sem, self.ld.n)
                self.st.inc(gpsimd.dma_start(out=self.yout[i * 128:(i + 1) * 128, :], in_=buf[i % 2]), 16)
            stored.append(self.st.n)
        al.pop()

    def final_wait(self):
        block = self.block

        @block.gpsimd
        def _(gpsimd):
            gpsimd.wait_ge(self.st.sem, self.st.n)
            gpsimd.wait_ge(self.gq.sem, self.gq.n)
            if self.upto >= 1:
                gpsimd.wait_ge(self.cc.sem, self.cc.n)


def build_nc(upto="F", debug=()):
    b = Builder(upto=upto, debug=debug)
    return b.build(), b.DBG


def kernel(**inputs):
    in_maps = _prep(inputs)
    nc, _ = build_nc(upto="F")
    res = run_bass_kernel_spmd(nc, in_maps, core_ids=list(range(NCORES)))
    y = np.concatenate([r["yout"] for r in res.results], axis=0)
    return np.ascontiguousarray(y).reshape(1, NCORES * TOK, D)
